# version 11
# baseline (speedup 1.0000x reference)
import numpy as np
import ml_dtypes
import concourse.bass as bass
import concourse.mybir as mybir
from concourse.bass_utils import run_bass_kernel_spmd

F32 = mybir.dt.float32
BF16 = mybir.dt.bfloat16
I16 = mybir.dt.int16
AF = mybir.ActivationFunctionType
ALU = mybir.AluOpType
AX = mybir.AxisListType

D_MODEL = 1024
PROJ = 4256
D_INNER = 2048
XBC = 2176
HIDDEN = 3072
T_CORE = 2048
NCH = 16
CH = 128
NTL = 4
TL = 512
NOC = 17
EPS = 1e-5
NEG = -10000.0

TAPS = ()
HARD_SYNC = False
SKIP_Z = False
SKIP_DT = False
SKIP_SSD = False
SKIP_HG = False
SKIP_GX = False
SKIP_VEY = False
SKIP_TEY = False
SKIP_VEY_Y = False
SKIP_VEY_H = False
DED_ACC = False
_TAP_SPECS = {
    "xct":    ([T_CORE, D_INNER], BF16),
    "bct":    ([T_CORE, 128], BF16),
    "dt":     ([T_CORE, 32], F32),
    "sz":     ([T_CORE, D_INNER], BF16),
    "y_loc":  ([T_CORE, D_INNER], BF16),
    "H_loc":  ([64, D_INNER], F32),
    "lgsum":  ([128, 32], F32),
    "hstart": ([64, D_INNER], F32),
    "y_full": ([T_CORE, D_INNER], BF16),
    "resid":  ([T_CORE, D_MODEL], F32),
    "lam":    ([128, NCH * 32], F32),
    "xcf":    ([T_CORE, NOC * 128], BF16),
    "xpre":   ([NTL * 128, NOC * 516], BF16),
    "negA":   ([128, 32], F32),
    "S":      ([T_CORE, 96], F32),
    "alpha":  ([T_CORE, 32], F32),
    "gam":    ([T_CORE, 32], F32),
}


class Prog:
    def __init__(self, nc):
        self.nc = nc
        self.sem = {}
        self.cnt = {}
        for n in ("qs", "qg", "te", "ve", "se", "ge"):
            self.sem[n] = nc.alloc_semaphore(n)
            self.cnt[n] = 0
        self.guard = {}

    def w(self, eng, name, lag=0):
        v = self.cnt[name] - lag
        if v > 0:
            eng.wait_ge(self.sem[name], v)

    def w_val(self, eng, name, val):
        if val > 0:
            eng.wait_ge(self.sem[name], val)

    def i(self, ins, name, k=1):
        self.cnt[name] += k
        return ins.then_inc(self.sem[name], k)

    def ps(self, eng, my_eng, bank):
        g = self.guard.get(bank)
        if g is not None and g[0] != my_eng and g[1] > 0:
            eng.wait_ge(self.sem[g[0]], g[1])

    def touch(self, bank, eng_name):
        self.guard[bank] = (eng_name, self.cnt[eng_name])


def build_program(taps=None):
    global HARD_SYNC
    if taps is None:
        taps = TAPS
    nc = bass.Bass("TRN2", target_bir_lowering=False, debug=False)
    nc.has_collectives = True

    x_ext = nc.dram_tensor("x_ext", [CH + T_CORE, D_MODEL], F32, kind="ExternalInput").ap()
    w_in = nc.dram_tensor("w_in", [D_MODEL, PROJ], F32, kind="ExternalInput").ap()
    w_out = nc.dram_tensor("w_out", [D_INNER, D_MODEL], F32, kind="ExternalInput").ap()
    convw_in = nc.dram_tensor("convw", [128, NOC, 4], F32, kind="ExternalInput").ap()
    convb_in = nc.dram_tensor("convb", [128, NOC], F32, kind="ExternalInput").ap()
    alog_in = nc.dram_tensor("alog", [128, 32], F32, kind="ExternalInput").ap()
    dtb_in = nc.dram_tensor("dtb", [128, 32], F32, kind="ExternalInput").ap()
    dpar_in = nc.dram_tensor("dpar", [128, 32], F32, kind="ExternalInput").ap()
    nw_in = nc.dram_tensor("nw", [128, D_INNER], F32, kind="ExternalInput").ap()
    cm_in = nc.dram_tensor("cm", [128, 16], F32, kind="ExternalInput").ap()
    cb_in = nc.dram_tensor("cb", [128, 4], F32, kind="ExternalInput").ap()
    ident_in = nc.dram_tensor("ident", [128, 128], BF16, kind="ExternalInput").ap()
    tri_in = nc.dram_tensor("tri", [128, 128], BF16, kind="ExternalInput").ap()
    us_in = nc.dram_tensor("us", [128, 128], BF16, kind="ExternalInput").ap()
    wg_t = nc.dram_tensor("wg_t", [D_MODEL, HIDDEN], I16, kind="ExternalInput").ap()
    wu_t = nc.dram_tensor("wu_t", [D_MODEL, HIDDEN], I16, kind="ExternalInput").ap()
    wd_t = nc.dram_tensor("wd_t", [HIDDEN, D_MODEL], I16, kind="ExternalInput").ap()

    out_hbm = nc.dram_tensor("out", [T_CORE, D_MODEL], F32, kind="ExternalOutput").ap()

    z_dram = nc.dram_tensor("z_dram", [NCH, 128, D_INNER], BF16, kind="Internal").ap()
    y_dram = nc.dram_tensor("y_dram", [NCH, 128, D_INNER], BF16, kind="Internal").ap()
    r_dram = nc.dram_tensor("r_dram", [T_CORE, D_MODEL], F32, kind="Internal").ap()
    wd_bf = nc.dram_tensor("wd_bf", [HIDDEN, D_MODEL], BF16, kind="Internal").ap()
    ag_in = nc.dram_tensor("ag_in", [64, D_INNER + 32], F32, kind="Internal").ap()
    ag_out = nc.dram_tensor("ag_out", [4, 64, D_INNER + 32], F32, kind="Internal").ap()

    tap_t = {}
    for name in taps:
        shp, dt_ = _TAP_SPECS[name]
        tap_t[name] = nc.dram_tensor("tap_" + name, shp, dt_, kind="ExternalOutput").ap()

    A = nc.alloc_sbuf_tensor
    arena_w = A("arena_w", [128, 49152], BF16)
    cv_acc = A("cv_acc", [128, NOC * 128], BF16)
    arena_x = A("arena_x", [128, 17664], BF16)
    x_tm = A("x_tm", [128, 2, 1024], F32)
    xn_fm = A("xn_fm", [128, 8, TL], BF16)
    sz_sb = A("sz_sb", [128, 2, D_INNER], BF16)
    xc_tm = A("xc_tm", [128, 2, D_INNER], BF16)
    bc_tm = A("bc_tm", [128, 2, 128], BF16)
    ct_all = A("ct_all", [64, NCH, 128], BF16)
    lam_all = A("lam_all", [128, NCH, 32], F32)
    lgsnap = A("lgsnap", [128, NCH, 32], BF16)
    lgcum = A("lgcum", [128, 32], F32)
    gam_sb = A("gam_sb", [128, 2, 32], F32)
    dt_sb = A("dt_sb", [128, 2, 32], BF16)
    dtw_sb = A("dtw_sb", [128, 2, 32], F32)
    al_bf = A("al_bf", [128, 2, 32], BF16)
    dtp_sb = A("dtp_sb", [128, 32], F32)
    expd = A("expd", [128, 2, 1024], BF16)
    m_t = A("m_t", [128, 2, 1024], BF16)
    gm_t = A("gm_t", [128, 2, 128], BF16)
    bw_sb = A("bw_sb", [128, 2, 512], BF16)
    H = A("H", [64, D_INNER], F32)
    H_bf = A("H_bf", [64, D_INNER], BF16)
    y_sb = A("y_sb", [128, D_INNER], BF16)
    y2_sb = A("y2_sb", [128, D_INNER], BF16)
    ssc = A("ssc", [128, 8], F32)
    c_ident = A("c_ident", [128, 128], BF16)
    c_tri = A("c_tri", [128, 128], BF16)
    c_us = A("c_us", [128, 128], BF16)
    c_convw = A("c_convw", [128, NOC, 4], F32)
    c_convb = A("c_convb", [128, NOC], F32)
    c_negA = A("c_negA", [128, 32], F32)
    c_dtb = A("c_dtb", [128, 32], F32)
    c_D = A("c_D", [128, 32], F32)
    c_eb = A("c_eb", [128, 2], F32)

    w_in_bf = arena_w.ap()[:, 0:8 * PROJ].rearrange("p (k c) -> p k c", k=8)
    wout_bf = arena_w.ap()[:, 0:16 * 1024].rearrange("p (k c) -> p k c", k=16)
    wg_bf = arena_w.ap()[:, 0:8 * HIDDEN].rearrange("p (k c) -> p k c", k=8)
    wu_bf = arena_w.ap()[:, 8 * HIDDEN:16 * HIDDEN].rearrange("p (k c) -> p k c", k=8)
    w_stage = arena_x.ap()[:, 0:8704].bitcast(F32)
    smr = arena_x.ap()[0:1, 8704:13312].bitcast(F32)
    xbc_pre = arena_x.ap()[:, 0:NOC * 516].rearrange("p (o c) -> p o c", o=NOC)
    xbc_cv = arena_x.ap()[:, NOC * 516:NOC * 516 + NOC * 256].rearrange(
        "p (b o c) -> p b o c", b=2, o=NOC)
    agst = arena_x.ap()[:, 0:256].bitcast(F32)
    w64 = arena_x.ap()[:, 256:512].bitcast(F32)
    sslot = arena_x.ap()[:, 512:4608].bitcast(F32)
    yn_fm = arena_x.ap()[:, 8448:10496]
    yn_fm3 = yn_fm.rearrange("p (k c) -> p k c", k=16)
    rsd_pp = arena_x.ap()[:, 10496:14592].bitcast(F32).rearrange("p (b c) -> p b c", b=2)
    h_sb = arena_x.ap()[:, 0:24 * TL].rearrange("p (o c) -> p o c", o=24)
    stage16 = arena_x.ap()[:, 14592:17664].bitcast(I16)
    c_nw = arena_x.ap()[:, 0:2048]
    g_sb = arena_x.ap()[:, 12288:14336]
    c_M = arena_x.ap()[:, 4608:4640].bitcast(F32)[:, 0:16]
    gp_tmp = arena_x.ap()[:, 13568:14080]
    c_cb = arena_x.ap()[:, 4640:4648].bitcast(F32)[:, 0:4]
    w_stage2 = arena_x.ap()[:, 14592:17664].bitcast(F32)
    wdb = y_sb.ap().rearrange("p (b c) -> p b c", b=2)

    P_ = nc.alloc_psum_tensor
    ps0 = P_("ps0", [128, 512], F32)
    ps1 = P_("ps1", [128, 512], F32)
    ps2 = P_("ps2", [128, 512], F32)
    ps3 = P_("ps3", [128, 512], F32)
    ps4 = P_("ps4", [128, 512], F32)
    ps5 = P_("ps5", [128, 512], F32)
    ps6 = P_("ps6", [128, 1024], BF16)
    ps7 = P_("ps7", [128, 512], F32)

    with nc.Block() as block:
        P = Prog(nc)
        sq, qg = "qs", "qg"

        def p0_sy(sync):
            for dst, src in ((c_ident, ident_in), (c_tri, tri_in), (c_us, us_in),
                             (c_convw, convw_in), (c_convb, convb_in),
                             (c_dtb, dtb_in), (c_D, dpar_in),
                             (c_negA, alog_in)):
                P.i(sync.dma_start(out=dst.ap()[:], in_=src), sq, 16)
        block.sync(p0_sy)



        def p0_se(se):
            P.w(se, sq)
            P.i(se.activation(out=dtp_sb.ap()[:], in_=c_negA.ap()[:], func=AF.Exp), "se")
        block.scalar(p0_se)

        def p0_ve2(ve):
            P.w(ve, "se")
            ve.tensor_scalar_mul(c_negA.ap()[:], dtp_sb.ap()[:], -1.0)
            ve.memzero(H.ap()[:])
            ve.memzero(H_bf.ap()[:])
            ve.memset(c_eb.ap()[:, 0:1], EPS)
            ve.memset(c_eb.ap()[:, 1:2], 1.0)
            P.i(ve.memzero(lgcum.ap()[:]), "ve")
        block.vector(p0_ve2)

        for k in range(8):
            def p1w_sy(sync, k=k):
                P.w(sync, "ve")
                P.i(sync.dma_start(out=w_stage[:, 0:PROJ], in_=w_in[k * 128:(k + 1) * 128, :]), sq, 16)
            block.sync(p1w_sy)

            def p1w_ve(ve, k=k):
                P.w(ve, sq)
                P.i(ve.tensor_copy(w_in_bf[:, k, :], w_stage[:, 0:PROJ]), "ve")
            block.vector(p1w_ve)

        def ln_chunk(ec, xbuf, fm_col, src_is_dram_resid=None, inv_scale=1.0):
            def sy(sync):
                P.w(sync, "ve")
                P.w(sync, "se")
                src = x_ext[ec * 128:(ec + 1) * 128, :] if src_is_dram_resid is None \
                    else r_dram[src_is_dram_resid * 128:(src_is_dram_resid + 1) * 128, :]
                P.i(sync.dma_start(out=x_tm.ap()[:, xbuf, :], in_=src), sq, 16)
            block.sync(sy)

            def se1(se):
                P.w(se, sq)
                P.w(se, "te")
                P.w(se, "ve")
                P.i(se.activation(out=cv_acc.ap()[:, 0:1024], in_=x_tm.ap()[:, xbuf, :], func=AF.Square,
                                  accum_out=ssc.ap()[:, 1:2]), "se")
            block.scalar(se1)

            def ve1(ve):
                P.w(ve, sq)
                P.i(ve.reduce_sum(ssc.ap()[:, 0:1], x_tm.ap()[:, xbuf, :], axis=AX.X), "ve")
                P.w(ve, "se")
                ve.tensor_scalar_mul(ssc.ap()[:, 2:3], ssc.ap()[:, 0:1], 1.0 / D_MODEL)
                ve.tensor_scalar_mul(ssc.ap()[:, 3:4], ssc.ap()[:, 1:2], 1.0 / D_MODEL)
                P.i(ve.scalar_tensor_tensor(ssc.ap()[:, 4:5], ssc.ap()[:, 2:3], ssc.ap()[:, 2:3],
                                            ssc.ap()[:, 3:4], op0=ALU.mult, op1=ALU.subtract), "ve")
            block.vector(ve1)

            def se2(se):
                P.w(se, "ve")
                se.activation(out=ssc.ap()[:, 5:6], in_=ssc.ap()[:, 4:5], func=AF.Ln, bias=c_eb.ap()[:, 0:1], scale=-1.0)
                P.i(se.activation(out=ssc.ap()[:, 6:7], in_=ssc.ap()[:, 5:6], func=AF.Exp, scale=-0.5), "se")
            block.scalar(se2)

            def ve2(ve):
                P.w(ve, "se")
                rst = ssc.ap()[:, 6:7]
                if inv_scale != 1.0:
                    ve.tensor_scalar_mul(ssc.ap()[:, 7:8], ssc.ap()[:, 6:7], inv_scale)
                    rst = ssc.ap()[:, 7:8]
                P.i(ve.tensor_scalar(cv_acc.ap()[:, 0:1024], x_tm.ap()[:, xbuf, :], ssc.ap()[:, 2:3],
                                     rst, op0=ALU.subtract, op1=ALU.mult), "ve")
            block.vector(ve2)

            def te1(te):
                P.w(te, "ve")
                P.ps(te, "te", "ps6")
                for k in range(8):
                    ins = te.transpose(ps6.ap()[:, k * 128:(k + 1) * 128],
                                       cv_acc.ap()[:, k * 128:(k + 1) * 128], c_ident.ap()[:])
                P.i(ins, "te")
            block.tensor(te1)
            P.touch("ps6", "te")

            def se3(se):
                P.w(se, "te")
                P.i(se.copy(out=xn_fm.ap()[:, :, fm_col:fm_col + 128],
                            in_=ps6.ap().rearrange("p (k c) -> p k c", k=8)), "se")
            block.scalar(se3)
            P.touch("ps6", "se")

        def xbc_mm(a, b, halo_only=False):
            for oc in range(NOC):
                bank = "ps0" if oc % 2 == 0 else "ps1"
                psx = ps0 if oc % 2 == 0 else ps1

                def temm(te, oc=oc, psx=psx, bank=bank):
                    P.ps(te, "te", bank)
                    P.w(te, "se")
                    for k in range(8):
                        ins = te.matmul(psx.ap()[:, 0:b - a],
                                        w_in_bf[:, k, D_INNER + oc * 128:D_INNER + (oc + 1) * 128],
                                        xn_fm.ap()[:, k, a:b], start=(k == 0), stop=(k == 7))
                    P.i(ins, "te")
                block.tensor(temm)
                P.touch(bank, "te")

                def seev(se, oc=oc, psx=psx, bank=bank):
                    P.w(se, "te")
                    if halo_only:
                        P.i(se.copy(out=xbc_pre[:, oc, 0:4], in_=psx.ap()[:, b - a - 4:b - a]), "se")
                    else:
                        P.i(se.copy(out=xbc_pre[:, oc, a + 4:b + 4], in_=psx.ap()[:, 0:b - a]), "se")
                block.scalar(seev)
                P.touch(bank, "se")

        ln_chunk(0, 0, 0)
        xbc_mm(0, 128, halo_only=True)

        frm_se = {}
        fz_gz = {}
        frm_dec = {}
        bk_vve = {}
        bk_gy = {}
        mcnt = {}

        def emit_frontend(c):
            tile, tc = c // 4, c % 4
            p = c % 2
            cvb = c % 2
            if tc == 0:
                for q in range(4):
                    ln_chunk(1 + tile * 4 + q, q % 2, q * 128)
                xbc_mm(0, TL)

            ztec = {}
            for zpair in range(2):
                for zc in (zpair * 2, zpair * 2 + 1):
                    bank = "ps2" if zc % 2 == 0 else "ps3"
                    psz = ps2 if zc % 2 == 0 else ps3

                    def tez(te, tc=tc, zc=zc, psz=psz, bank=bank):
                        P.ps(te, "te", bank)
                        for k in range(8):
                            ins = te.matmul(psz.ap()[:], xn_fm.ap()[:, k, tc * 128:(tc + 1) * 128],
                                            w_in_bf[:, k, zc * 512:(zc + 1) * 512],
                                            start=(k == 0), stop=(k == 7))
                        P.i(ins, "te")
                        ztec[zc] = P.cnt["te"]
                    block.tensor(tez)
                    P.touch(bank, "te")

                def vzev(ve, p=p, c=c, zpair=zpair):
                    for zc in (zpair * 2, zpair * 2 + 1):
                        psz = ps2 if zc % 2 == 0 else ps3
                        P.w_val(ve, "te", ztec[zc])
                        if zc == 0 and c >= 1:
                            P.w_val(ve, qg, fz_gz.get(c - 1, 0))
                        ins = ve.tensor_copy(sz_sb.ap()[:, 1 - p, zc * 512:(zc + 1) * 512], psz.ap()[:])
                    P.i(ins, "ve")
                block.vector(vzev)
                P.touch("ps2", "ve")
                P.touch("ps3", "ve")

            def tedt(te, tc=tc):
                P.ps(te, "te", "ps5")
                for k in range(8):
                    ins = te.matmul(ps5.ap()[:, 96:128], xn_fm.ap()[:, k, tc * 128:(tc + 1) * 128],
                                    w_in_bf[:, k, 4224:4256], start=(k == 0), stop=(k == 7))
                P.i(ins, "te")
            block.tensor(tedt)
            P.touch("ps5", "te")

            def vdt(ve):
                P.w(ve, "te")
                P.i(ve.tensor_add(dtp_sb.ap()[:], ps5.ap()[:, 96:128], c_dtb.ap()[:]), "ve")
            block.vector(vdt)
            P.touch("ps5", "ve")

            def sdt(se, p=p):
                P.w(se, "ve")
                se.activation(out=dtp_sb.ap()[:], in_=dtp_sb.ap()[:], func=AF.Exp)
                P.i(se.activation(out=dt_sb.ap()[:, p, :], in_=dtp_sb.ap()[:], func=AF.Ln,
                                  bias=c_eb.ap()[:, 1:2]), "se")
            block.scalar(sdt)

            def valpha(ve, p=p, c=c):
                P.w(ve, "se")
                if c >= 2:
                    P.w_val(ve, "te", frm_dec.get(c - 2, 0))
                P.i(ve.tensor_tensor(al_bf.ap()[:, p, :], dt_sb.ap()[:, p, :], c_negA.ap()[:],
                                     op=ALU.mult), "ve")
            block.vector(valpha)

            def tedec(te, p=p):
                P.ps(te, "te", "ps5")
                P.w(te, "ve")
                te.matmul(ps5.ap()[:, 0:32], c_tri.ap()[:], al_bf.ap()[:, p, :], start=True, stop=True)
                P.i(te.matmul(ps5.ap()[:, 32:64], c_us.ap()[:], al_bf.ap()[:, p, :], start=True, stop=True), "te")
                frm_dec[c] = P.cnt["te"]
            block.tensor(tedec)
            P.touch("ps5", "te")

            def sdec(se, c=c):
                P.w(se, "te")
                se.activation(out=lam_all.ap()[:, c, :], in_=ps5.ap()[:, 0:32], func=AF.Exp)
                P.i(se.activation(out=dtp_sb.ap()[:], in_=ps5.ap()[:, 32:64], func=AF.Exp), "se")
            block.scalar(sdec)

            def vdec(ve, c=c, p=p):
                P.w(ve, "se")
                ve.tensor_copy(lgsnap.ap()[:, c, :], lgcum.ap()[:])
                ve.tensor_add(lgcum.ap()[:], lgcum.ap()[:], ps5.ap()[:, 0:32])
                ve.tensor_add(lgcum.ap()[:], lgcum.ap()[:], ps5.ap()[:, 32:64])
                ve.tensor_mul(gam_sb.ap()[:, p, :], lam_all.ap()[:, c, :], dtp_sb.ap()[:])
                P.i(ve.tensor_mul(dtw_sb.ap()[:, p, :], dt_sb.ap()[:, p, :], dtp_sb.ap()[:]), "ve")
            block.vector(vdec)
            P.touch("ps5", "ve")

            NOC_VE = 12

            def vconv_ve(ve, tc=tc, p=p):
                P.w(ve, "se")
                base = tc * 128
                for oc in range(NOC_VE):
                    acc = cv_acc.ap()[:, oc * 128:(oc + 1) * 128]
                    ve.tensor_tensor(acc, xbc_pre[:, oc, base + 1:base + 129],
                                     c_convw.ap()[:, oc, 0:1].broadcast_to([128, 128]), op=ALU.mult)
                    for kk in range(1, 4):
                        ve.tensor_tensor(bc_tm.ap()[:, p, :], xbc_pre[:, oc, base + 1 + kk:base + 129 + kk],
                                         c_convw.ap()[:, oc, kk:kk + 1].broadcast_to([128, 128]), op=ALU.mult)
                        ve.tensor_tensor(acc, acc, bc_tm.ap()[:, p, :], op=ALU.add)
                P.i(ve.nop(), "ve")
            block.vector(vconv_ve)

            def vconv_gp(g, tc=tc):
                P.w(g, "se")
                P.w(g, "te")
                base = tc * 128
                for oc in range(NOC_VE, NOC):
                    acc = cv_acc.ap()[:, oc * 128:(oc + 1) * 128]
                    g.tensor_tensor(acc, xbc_pre[:, oc, base + 1:base + 129],
                                    c_convw.ap()[:, oc, 0:1].broadcast_to([128, 128]), op=ALU.mult)
                    for kk in range(1, 4):
                        g.tensor_tensor(gp_tmp[:, 0:128], xbc_pre[:, oc, base + 1 + kk:base + 129 + kk],
                                        c_convw.ap()[:, oc, kk:kk + 1].broadcast_to([128, 128]), op=ALU.mult)
                        g.tensor_tensor(acc, acc, gp_tmp[:, 0:128], op=ALU.add)
                P.i(g.nop(), "ge")
            block.gpsimd(vconv_gp)

            def sconv(se, cvb=cvb, p=p, c=c):
                P.w(se, "ge")
                P.w(se, "ve")
                P.w(se, "te")
                if c >= 2:
                    P.w_val(se, qg, bk_gy.get(c - 2, 0))
                se.activation(out=sz_sb.ap()[:, p, :], in_=sz_sb.ap()[:, 1 - p, :], func=AF.Silu)
                for oc in range(NOC):
                    ins = se.activation(out=xbc_cv[:, cvb, oc, :], in_=cv_acc.ap()[:, oc * 128:(oc + 1) * 128],
                                        func=AF.Silu, bias=c_convb.ap()[:, oc:oc + 1])
                P.i(ins, "se")
            block.scalar(sconv)

            def gz(g, c=c, p=p):
                P.w(g, "se")
                P.i(g.dma_start(out=z_dram[c], in_=sz_sb.ap()[:, p, :]), qg, 16)
                fz_gz[c] = P.cnt[qg]
            block.gpsimd(gz)

            for half in range(2):
                def tetr(te, half=half, cvb=cvb):
                    P.w(te, "se")
                    P.ps(te, "te", "ps6")
                    for q in range(8):
                        ins = te.transpose(ps6.ap()[:, q * 128:(q + 1) * 128],
                                           xbc_cv[:, cvb, half * 8 + q, :], c_ident.ap()[:])
                    P.i(ins, "te")
                block.tensor(tetr)
                P.touch("ps6", "te")

                def setr(se, half=half, p=p):
                    P.w(se, "te")
                    P.w(se, "ve")
                    P.w(se, "ge")
                    P.i(se.copy(out=xc_tm.ap()[:, p, half * 1024:(half + 1) * 1024], in_=ps6.ap()[:]), "se")
                block.scalar(setr)
                P.touch("ps6", "se")

            def tebc(te, cvb=cvb):
                P.w(te, "se")
                P.ps(te, "te", "ps6")
                P.i(te.transpose(ps6.ap()[:, 0:128], xbc_cv[:, cvb, 16, :], c_ident.ap()[:]), "te")
            block.tensor(tebc)
            P.touch("ps6", "te")

            def sebc(se, p=p):
                P.w(se, "te")
                P.w(se, "ve")
                P.i(se.copy(out=bc_tm.ap()[:, p, :], in_=ps6.ap()[:, 0:128]), "se")
            block.scalar(sebc)
            P.touch("ps6", "se")

            def tect(te, p=p):
                P.w(te, "se")
                P.ps(te, "te", "ps6")
                P.i(te.transpose(ps6.ap()[0:64, 128:256], bc_tm.ap()[:, p, 64:128], c_ident.ap()[:]), "te")
            block.tensor(tect)
            P.touch("ps6", "te")

            def sect(se, c=c):
                P.w(se, "te")
                P.i(se.copy(out=ct_all.ap()[:, c, :], in_=ps6.ap()[0:64, 128:256]), "se")
            block.scalar(sect)
            P.touch("ps6", "se")

            def teg(te, c=c, cvb=cvb):
                P.ps(te, "te", "ps5")
                P.w(te, "se")
                P.i(te.matmul(ps5.ap()[:, 128:256], xbc_cv[0:64, cvb, 16, :],
                              ct_all.ap()[:, c, :], start=True, stop=True), "te")
            block.tensor(teg)
            P.touch("ps5", "te")

            def veg(ve, p=p):
                P.w(ve, "te")
                P.i(ve.tensor_tensor(gm_t.ap()[:, p, :], ps5.ap()[:, 128:256], c_tri.ap()[:], op=ALU.mult), "ve")
                frm_se[c] = None
            block.vector(veg)
            P.touch("ps5", "ve")
            frm_se[c] = P.cnt["se"]

            if tc == 3 and tile < NTL - 1:
                def vhalo(ve):
                    P.w(ve, "se")
                    P.i(ve.tensor_copy(xbc_pre[:, :, 0:4], xbc_pre[:, :, 512:516]), "ve")
                block.vector(vhalo)

        def emit_backend(c):
            p = c % 2

            def em_W(hg):
                def f(ve, hg=hg):
                    P.w_val(ve, "se", frm_se[c])
                    b = hg % 2
                    ve.tensor_tensor(
                        expd.ap()[:, b, :].rearrange("p (h t) -> p h t", h=8),
                        al_bf.ap()[:, p, hg * 8:(hg + 1) * 8].unsqueeze(2).broadcast_to([128, 8, 128]),
                        c_tri.ap().unsqueeze(1).broadcast_to([128, 8, 128]),
                        op=ALU.mult)
                    P.i(ve.tensor_tensor(
                        bw_sb.ap()[:, b, :].rearrange("p (h n) -> p h n", h=8),
                        bc_tm.ap()[:, p, 0:64].unsqueeze(1).broadcast_to([128, 8, 64]),
                        dtw_sb.ap()[:, p, hg * 8:(hg + 1) * 8].unsqueeze(2).broadcast_to([128, 8, 64]),
                        op=ALU.mult), "ve")
                    mcnt[("W", hg)] = P.cnt["ve"]
                block.vector(f)

            def em_D(hg):
                def f(te, hg=hg):
                    b = hg % 2
                    P.w_val(te, "ve", mcnt[("W", hg)])
                    P.ps(te, "te", "ps7")
                    te.matmul(ps7.ap()[:], c_us.ap()[:], expd.ap()[:, b, 0:512], start=True, stop=True)
                    P.ps(te, "te", "ps4")
                    P.i(te.matmul(ps4.ap()[:], c_us.ap()[:], expd.ap()[:, b, 512:1024],
                                  start=True, stop=True), "te")
                    mcnt[("D", hg)] = P.cnt["te"]
                block.tensor(f)
                P.touch("ps7", "te"); P.touch("ps4", "te")

            def em_E(hg):
                def f(se, hg=hg):
                    b = hg % 2
                    P.w_val(se, "te", mcnt[("D", hg)])
                    se.activation(out=expd.ap()[:, b, 0:512], in_=ps7.ap()[:], func=AF.Exp)
                    P.i(se.activation(out=expd.ap()[:, b, 512:1024], in_=ps4.ap()[:], func=AF.Exp), "se")
                    mcnt[("E", hg)] = P.cnt["se"]
                block.scalar(f)
                P.touch("ps7", "se"); P.touch("ps4", "se")

            def em_M(hg):
                def f(ve, hg=hg):
                    b = hg % 2
                    P.w_val(ve, "se", mcnt[("E", hg)])
                    for q in range(8):
                        ins = ve.scalar_tensor_tensor(
                            m_t.ap()[:, b, q * 128:(q + 1) * 128],
                            expd.ap()[:, b, q * 128:(q + 1) * 128],
                            dt_sb.ap()[:, p, hg * 8 + q:hg * 8 + q + 1],
                            gm_t.ap()[:, p, :], op0=ALU.mult, op1=ALU.mult)
                    P.i(ins, "ve")
                    mcnt[("M", hg)] = P.cnt["ve"]
                block.vector(f)

            def em_Y(hg):
                def f(te, hg=hg):
                    b = hg % 2
                    P.w_val(te, "ve", mcnt[("M", hg)])
                    if hg == 0:
                        P.w_val(te, "se", frm_se[c])
                    if c >= 1:
                        P.w_val(te, "ve", bk_vve.get((c - 1, hg), 0))
                    P.ps(te, "te", "ps0")
                    P.ps(te, "te", "ps1")
                    P.ps(te, "te", "ps4")
                    for q in range(8):
                        h = hg * 8 + q
                        te.matmul(ps0.ap()[:, q * 64:(q + 1) * 64], m_t.ap()[:, b, q * 128:(q + 1) * 128],
                                  xc_tm.ap()[:, p, h * 64:(h + 1) * 64], start=True, stop=True)
                    for q in range(8):
                        h = hg * 8 + q
                        te.matmul(ps1.ap()[:, q * 64:(q + 1) * 64], ct_all.ap()[:, c, :],
                                  H_bf.ap()[:, h * 64:(h + 1) * 64], start=True, stop=True)
                    for q in range(8):
                        h = hg * 8 + q
                        ins = te.matmul(ps4.ap()[0:64, q * 64:(q + 1) * 64],
                                        bw_sb.ap()[:, hg % 2, q * 64:(q + 1) * 64],
                                        xc_tm.ap()[:, p, h * 64:(h + 1) * 64], start=True, stop=True)
                    P.i(ins, "te")
                    mcnt[("Y", hg)] = P.cnt["te"]
                block.tensor(f)
                P.touch("ps0", "te"); P.touch("ps1", "te"); P.touch("ps4", "te")

            def em_Vse(hg):
                def f(se, hg=hg):
                    P.w_val(se, "te", mcnt[("Y", hg)])
                    if hg == 0 and c >= 1:
                        P.w_val(se, qg, bk_gy.get(c - 1, 0))
                    P.i(se.copy(out=y_sb.ap()[:, hg * 512:(hg + 1) * 512], in_=ps0.ap()[:]), "se")
                    mcnt[("Vse", hg)] = P.cnt["se"]
                block.scalar(f)
                P.touch("ps0", "se")

            def em_Vve(hg):
                def f(ve, hg=hg):
                    P.w_val(ve, "te", mcnt[("Y", hg)])
                    P.w_val(ve, "se", mcnt[("Vse", hg)])
                    sl = slice(hg * 512, (hg + 1) * 512)
                    ve.tensor_tensor(y2_sb.ap()[:, sl].rearrange("p (h d) -> p h d", h=8),
                                     ps1.ap().rearrange("p (h d) -> p h d", h=8),
                                     lam_all.ap()[:, c, hg * 8:(hg + 1) * 8].unsqueeze(2).broadcast_to([128, 8, 64]),
                                     op=ALU.mult)
                    P.i(ve.tensor_add(y_sb.ap()[:, sl], y_sb.ap()[:, sl], y2_sb.ap()[:, sl]), "ve")
                    mcnt[("Vy", hg)] = P.cnt["ve"]
                    ve.tensor_tensor(H.ap()[:, sl].rearrange("p (h d) -> p h d", h=8),
                                     H.ap()[:, sl].rearrange("p (h d) -> p h d", h=8),
                                     gam_sb.ap()[0:64, p, hg * 8:(hg + 1) * 8].unsqueeze(2).broadcast_to([64, 8, 64]),
                                     op=ALU.mult)
                    ve.tensor_add(H.ap()[:, sl], H.ap()[:, sl], ps4.ap()[0:64, :])
                    P.i(ve.tensor_copy(H_bf.ap()[:, sl], H.ap()[:, sl]), "ve")
                    bk_vve[(c, hg)] = P.cnt["ve"]
                block.vector(f)
                P.touch("ps1", "ve"); P.touch("ps4", "ve")

            def em_Vgp(hg):
                def f(g, hg=hg):
                    sl = slice(hg * 512, (hg + 1) * 512)
                    P.w_val(g, "ve", mcnt[("Vy", hg)])
                    g.tensor_tensor(gp_tmp.rearrange("p (h d) -> p h d", h=8),
                                    xc_tm.ap()[:, p, sl].rearrange("p (h d) -> p h d", h=8),
                                    c_D.ap()[:, hg * 8:(hg + 1) * 8].unsqueeze(2).broadcast_to([128, 8, 64]),
                                    op=ALU.mult)
                    P.i(g.tensor_tensor(y_sb.ap()[:, sl], y_sb.ap()[:, sl], gp_tmp, op=ALU.add), "ge")
                    mcnt[("Vgp", hg)] = P.cnt["ge"]
                block.gpsimd(f)

            em_W(0); em_D(0); em_E(0)
            for hg in range(4):
                em_M(hg)
                if hg + 1 < 4:
                    em_W(hg + 1)
                em_Y(hg)
                em_Vse(hg)
                em_Vve(hg)
                if hg + 1 < 4:
                    em_D(hg + 1)
                    em_E(hg + 1)
                em_Vgp(hg)

            def gy(g, c=c):
                P.w_val(g, "ve", bk_vve[(c, 3)])
                P.i(g.dma_start(out=y_dram[c], in_=y_sb.ap()[:]), qg, 16)
                bk_gy[c] = P.cnt[qg]
            block.gpsimd(gy)

        for c in range(NCH):
            emit_frontend(c)
            if c >= 1:
                emit_backend(c - 1)
        emit_backend(NCH - 1)

        for k in range(16):
            def p3w_sy(sync, k=k):
                P.w(sync, "ve")
                if k == 0:
                    P.w(sync, "te")
                P.i(sync.dma_start(out=w_stage2[:, 0:1024], in_=w_out[k * 128:(k + 1) * 128, :]), sq, 16)
            block.sync(p3w_sy)

            def p3w_ve(ve, k=k):
                P.w(ve, sq)
                P.i(ve.tensor_copy(wout_bf[:, k, :], w_stage2[:, 0:1024]), "ve")
            block.vector(p3w_ve)

        def p2_sy(sync):
            P.w(sync, "ve")
            P.i(sync.dma_start(out=ag_in[:, 0:D_INNER], in_=H.ap()[:]), sq, 16)
            P.i(sync.dma_start(out=ag_in[:, D_INNER:D_INNER + 32], in_=lgcum.ap()[0:64, :]), sq, 16)
        block.sync(p2_sy)

        def p2_cc(g):
            P.w(g, sq)
            P.w(g, qg)
            P.i(g.collective_compute("AllGather", ALU.bypass, ins=[ag_in], outs=[ag_out],
                                     replica_groups=[[0, 1, 2, 3], [4, 5, 6, 7]]), "ge")
        block.gpsimd(p2_cc)

        def p2_sy2(sync):
            P.w(sync, "ge")
            P.i(sync.dma_start(out=agst[0:64, 0:128].rearrange("p (i h) -> p i h", i=4),
                               in_=ag_out[:, :, D_INNER:D_INNER + 32].rearrange("i p h -> p i h")), sq, 16)
            P.i(sync.dma_start(out=c_M, in_=cm_in), sq, 16)
            P.i(sync.dma_start(out=c_cb, in_=cb_in), sq, 16)
        block.sync(p2_sy2)

        def p2_ve(ve):
            P.w(ve, sq)
            for i in range(4):
                ve.tensor_scalar_mul(w64[0:64, i * 32:(i + 1) * 32], agst[0:64, 0:32],
                                     c_M[0:64, i * 4:i * 4 + 1])
                for k in range(1, 4):
                    ve.scalar_tensor_tensor(w64[0:64, i * 32:(i + 1) * 32], agst[0:64, k * 32:(k + 1) * 32],
                                            c_M[0:64, i * 4 + k:i * 4 + k + 1],
                                            w64[0:64, i * 32:(i + 1) * 32], op0=ALU.mult, op1=ALU.add)
                ve.tensor_scalar_add(w64[0:64, i * 32:(i + 1) * 32], w64[0:64, i * 32:(i + 1) * 32],
                                     c_cb[0:64, i:i + 1])
            P.i(ve.engine_nop(), "ve")
        block.vector(p2_ve)

        def p2_se(se):
            P.w(se, "ve")
            P.i(se.activation(out=agst[0:64, 0:128], in_=w64[0:64, :], func=AF.Exp), "se")
        block.scalar(p2_se)

        def p2_vz(ve):
            P.w(ve, "se")
            P.i(ve.memzero(H.ap()[:]), "ve")
        block.vector(p2_vz)

        for i in range(4):
            def p2_syl(sync, i=i):
                P.w(sync, "ve")
                P.i(sync.dma_start(out=sslot[0:64, 0:D_INNER], in_=ag_out[i, :, 0:D_INNER]), sq, 16)
            block.sync(p2_syl)

            def p2_vel(ve, i=i):
                P.w(ve, sq)
                ve.tensor_tensor(sslot[0:64, 0:D_INNER].rearrange("p (h d) -> p h d", h=32),
                                 sslot[0:64, 0:D_INNER].rearrange("p (h d) -> p h d", h=32),
                                 agst[0:64, i * 32:(i + 1) * 32].unsqueeze(2).broadcast_to([64, 32, 64]),
                                 op=ALU.mult)
                P.i(ve.tensor_add(H.ap()[:], H.ap()[:], sslot[0:64, 0:D_INNER]), "ve")
            block.vector(p2_vel)

        def p2_vbf(ve):
            P.i(ve.tensor_copy(H_bf.ap()[:], H.ap()[:]), "ve")
        block.vector(p2_vbf)

        if "hstart" in tap_t:
            def thstart(sync):
                P.w(sync, "ve")
                P.i(sync.dma_start(out=tap_t["hstart"], in_=H.ap()[:]), sq, 16)
            block.sync(thstart)



        nw_done = [0]

        def p3_nw(g):
            P.w(g, "ve")
            P.i(g.dma_start(out=c_nw, in_=nw_in), qg, 16)
            nw_done[0] = P.cnt[qg]
        block.gpsimd(p3_nw)

        for k in range(24):
            def p3wd(g, k=k):
                if k == 0:
                    P.w(g, "ve")
                P.i(g.dma_start(out=stage16[:, 0:1024], in_=wd_t[k * 128:(k + 1) * 128, :]), qg, 16)
                P.w(g, qg)
                for hf in range(2):
                    P.i(g.tensor_copy(cv_acc.ap()[:, 0:512], stage16[:, hf * 512:(hf + 1) * 512]), "ge")
                    P.i(g.dma_start(out=wd_bf[k * 128:(k + 1) * 128, hf * 512:(hf + 1) * 512],
                                    in_=cv_acc.ap()[:, 0:512]), qg, 16)
                    P.w(g, qg)
            block.gpsimd(p3wd)

        for c in range(NCH):
            yb = c % 2

            def p3_syc(sync, c=c, yb=yb):
                P.w(sync, "ve")
                P.w(sync, "te")
                P.i(sync.dma_start(out=y_sb.ap()[:], in_=y_dram[c]), sq, 16)
                P.i(sync.dma_start(out=sz_sb.ap()[:, yb, :], in_=z_dram[c]), sq, 16)
                P.i(sync.dma_start(out=x_tm.ap()[:, yb, :], in_=x_ext[(c + 1) * 128:(c + 2) * 128, :]), sq, 16)
            block.sync(p3_syc)

            def p3_slam(se, c=c):
                P.w(se, "ve")
                P.i(se.activation(out=dtp_sb.ap()[:], in_=lgsnap.ap()[:, c, :], func=AF.Exp), "se")
            block.scalar(p3_slam)

            def p3_vlam(ve, c=c):
                P.w(ve, "se")
                P.i(ve.tensor_mul(gam_sb.ap()[:, 0, :], lam_all.ap()[:, c, :], dtp_sb.ap()[:]), "ve")
            block.vector(p3_vlam)

            for hg in range(4):
                def p3_tec(te, c=c, hg=hg):
                    P.ps(te, "te", "ps1")
                    if hg == 0:
                        P.w(te, "ve")
                    for q in range(8):
                        h = hg * 8 + q
                        ins = te.matmul(ps1.ap()[:, q * 64:(q + 1) * 64], ct_all.ap()[:, c, :],
                                        H_bf.ap()[:, h * 64:(h + 1) * 64], start=True, stop=True)
                    P.i(ins, "te")
                block.tensor(p3_tec)
                P.touch("ps1", "te")

                def p3_vec(ve, c=c, hg=hg):
                    P.w(ve, "te")
                    P.w(ve, sq)
                    sl = slice(hg * 512, (hg + 1) * 512)
                    ve.tensor_tensor(y2_sb.ap()[:, sl].rearrange("p (h d) -> p h d", h=8),
                                     ps1.ap().rearrange("p (h d) -> p h d", h=8),
                                     gam_sb.ap()[:, 0, hg * 8:(hg + 1) * 8].unsqueeze(2).broadcast_to([128, 8, 64]),
                                     op=ALU.mult)
                    P.i(ve.tensor_add(y_sb.ap()[:, sl], y_sb.ap()[:, sl], y2_sb.ap()[:, sl]), "ve")
                block.vector(p3_vec)
                P.touch("ps1", "ve")

            if "y_full" in tap_t:
                def tyf(sync, c=c):
                    P.w(sync, "ve")
                    P.i(sync.dma_start(out=tap_t["y_full"][c * 128:(c + 1) * 128, :], in_=y_sb.ap()[:]), sq, 16)
                block.sync(tyf)

            def p3_vg(ve, yb=yb):
                P.i(ve.tensor_mul(y_sb.ap()[:], y_sb.ap()[:], sz_sb.ap()[:, yb, :]), "ve")
            block.vector(p3_vg)

            def p3_srms(se):
                P.w(se, "ve")
                se.activation(out=y2_sb.ap()[:, 0:1024], in_=y_sb.ap()[:, 0:1024], func=AF.Square,
                              accum_out=ssc.ap()[:, 0:1])
                P.i(se.activation(out=y2_sb.ap()[:, 1024:2048], in_=y_sb.ap()[:, 1024:2048], func=AF.Square,
                                  accum_out=ssc.ap()[:, 1:2]), "se")
            block.scalar(p3_srms)

            def p3_vrms(ve):
                P.w(ve, "se")
                P.i(ve.tensor_add(ssc.ap()[:, 2:3], ssc.ap()[:, 0:1], ssc.ap()[:, 1:2]), "ve")
            block.vector(p3_vrms)

            def p3_srstd(se):
                P.w(se, "ve")
                se.activation(out=ssc.ap()[:, 3:4], in_=ssc.ap()[:, 2:3], func=AF.Ln,
                              bias=c_eb.ap()[:, 0:1], scale=1.0 / D_INNER)
                P.i(se.activation(out=ssc.ap()[:, 4:5], in_=ssc.ap()[:, 3:4], func=AF.Exp, scale=-0.5), "se")
            block.scalar(p3_srstd)

            def p3_vyn(ve, c=c):
                P.w(ve, "se")
                if c == 0:
                    P.w_val(ve, qg, nw_done[0])
                P.i(ve.scalar_tensor_tensor(y_sb.ap()[:], y_sb.ap()[:], ssc.ap()[:, 4:5], c_nw,
                                            op0=ALU.mult, op1=ALU.mult), "ve")
            block.vector(p3_vyn)

            for half in range(2):
                def p3_tet(te, half=half):
                    P.w(te, "ve")
                    P.ps(te, "te", "ps6")
                    for q in range(8):
                        ins = te.transpose(ps6.ap()[:, q * 128:(q + 1) * 128],
                                           y_sb.ap()[:, (half * 8 + q) * 128:(half * 8 + q + 1) * 128],
                                           c_ident.ap()[:])
                    P.i(ins, "te")
                block.tensor(p3_tet)
                P.touch("ps6", "te")

                def p3_set(se, half=half):
                    P.w(se, "te")
                    P.i(se.copy(out=yn_fm[:, half * 1024:(half + 1) * 1024], in_=ps6.ap()[:]), "se")
                block.scalar(p3_set)
                P.touch("ps6", "se")

            def p3_teo(te):
                P.w(te, "se")
                P.ps(te, "te", "ps2")
                P.ps(te, "te", "ps3")
                for k in range(16):
                    te.matmul(ps2.ap()[:], yn_fm3[:, k, :], wout_bf[:, k, 0:512], start=(k == 0), stop=(k == 15))
                for k in range(16):
                    ins = te.matmul(ps3.ap()[:], yn_fm3[:, k, :], wout_bf[:, k, 512:1024],
                                    start=(k == 0), stop=(k == 15))
                P.i(ins, "te")
            block.tensor(p3_teo)
            P.touch("ps2", "te"); P.touch("ps3", "te")

            def p3_vres(ve, yb=yb):
                P.w(ve, "te")
                ve.tensor_add(rsd_pp[:, yb, 0:512], ps2.ap()[:], x_tm.ap()[:, yb, 0:512])
                P.i(ve.tensor_add(rsd_pp[:, yb, 512:1024], ps3.ap()[:], x_tm.ap()[:, yb, 512:1024]), "ve")
            block.vector(p3_vres)
            P.touch("ps2", "ve"); P.touch("ps3", "ve")

            def p3_syr(sync, c=c, yb=yb):
                P.w(sync, "ve")
                P.i(sync.dma_start(out=r_dram[c * 128:(c + 1) * 128, :], in_=rsd_pp[:, yb, :]), sq, 16)
            block.sync(p3_syr)

        for wsrc, wdst in ((wg_t, wg_bf), (wu_t, wu_bf)):
            for k in range(8):
                def p4w_sy(sync, wsrc=wsrc, k=k):
                    P.w(sync, "ve")
                    P.w(sync, "ge")
                    P.i(sync.dma_start(out=stage16[:, 0:HIDDEN], in_=wsrc[k * 128:(k + 1) * 128, :]), sq, 16)
                block.sync(p4w_sy)

                def p4w_ve(ve, wdst=wdst, k=k):
                    P.w(ve, sq)
                    P.i(ve.tensor_copy(wdst[:, k, :], stage16[:, 0:HIDDEN]), "ve")
                block.vector(p4w_ve)

        first_down = [True]
        for tile in range(NTL):
            for tc in range(4):
                c = tile * 4 + tc
                ln_chunk(0, tc % 2, tc * 128, src_is_dram_resid=c, inv_scale=1.0 / 127.0)

            for hc in range(24):
                gb, gbn = (ps0, "ps0") if hc % 2 == 0 else (ps4, "ps4")
                ub, ubn = (ps1, "ps1") if hc % 2 == 0 else (ps5, "ps5")

                def p4_teg(te, hc=hc, gb=gb, ub=ub, gbn=gbn, ubn=ubn):
                    if hc == 0:
                        P.w(te, "se")
                    P.ps(te, "te", gbn)
                    P.ps(te, "te", ubn)
                    for k in range(8):
                        te.matmul(gb.ap()[:], wg_bf[:, k, hc * 128:(hc + 1) * 128], xn_fm.ap()[:, k, :],
                                  start=(k == 0), stop=(k == 7))
                    for k in range(8):
                        ins = te.matmul(ub.ap()[:], wu_bf[:, k, hc * 128:(hc + 1) * 128], xn_fm.ap()[:, k, :],
                                        start=(k == 0), stop=(k == 7))
                    P.i(ins, "te")
                block.tensor(p4_teg)
                P.touch(gbn, "te"); P.touch(ubn, "te")

                def p4_seg(se, hc=hc, gb=gb):
                    P.w(se, "te")
                    P.i(se.activation(out=g_sb[:, (hc % 2) * 512:(hc % 2 + 1) * 512], in_=gb.ap()[:], func=AF.Silu), "se")
                block.scalar(p4_seg)
                P.touch(gbn, "se")

                def p4_veh(ve, hc=hc, ub=ub):
                    P.w(ve, "se")
                    P.i(ve.scalar_tensor_tensor(h_sb[:, hc, :], ub.ap()[:], 1.0 / 127.0, g_sb[:, (hc % 2) * 512:(hc % 2 + 1) * 512],
                                                op0=ALU.mult, op1=ALU.mult), "ve")
                block.vector(p4_veh)
                P.touch(ubn, "ve")

            for tch in range(2):
                tca, tcb = tch * 2, tch * 2 + 1
                ca, cb = tile * 4 + tca, tile * 4 + tcb

                def p4_syd0(sync, tch=tch):
                    if first_down[0]:
                        P.w(sync, qg)
                        P.w(sync, "ge")
                        first_down[0] = False
                    P.w(sync, "ve")
                    P.w(sync, "te")
                    P.i(sync.dma_start(out=wdb[:, 0, :], in_=wd_bf[0:128, :]), sq, 16)
                block.sync(p4_syd0)

                for hc in range(24):
                    if hc + 1 < 24:
                        def p4_syd(sync, hc=hc):
                            P.w(sync, "te")
                            P.i(sync.dma_start(out=wdb[:, (hc + 1) % 2, :],
                                               in_=wd_bf[(hc + 1) * 128:(hc + 2) * 128, :]), sq, 16)
                        block.sync(p4_syd)

                    def p4_ted(te, hc=hc, tca=tca, tcb=tcb):
                        P.w(te, sq, 16 if hc + 1 < 24 else 0)
                        if hc == 0:
                            P.ps(te, "te", "ps2"); P.ps(te, "te", "ps3")
                            P.ps(te, "te", "ps4"); P.ps(te, "te", "ps5")
                        st_, sp_ = (hc == 0), (hc == 23)
                        te.matmul(ps2.ap()[:], h_sb[:, hc, tca * 128:(tca + 1) * 128], wdb[:, hc % 2, 0:512],
                                  start=st_, stop=sp_)
                        te.matmul(ps3.ap()[:], h_sb[:, hc, tca * 128:(tca + 1) * 128], wdb[:, hc % 2, 512:1024],
                                  start=st_, stop=sp_)
                        te.matmul(ps4.ap()[:], h_sb[:, hc, tcb * 128:(tcb + 1) * 128], wdb[:, hc % 2, 0:512],
                                  start=st_, stop=sp_)
                        ins = te.matmul(ps5.ap()[:], h_sb[:, hc, tcb * 128:(tcb + 1) * 128], wdb[:, hc % 2, 512:1024],
                                        start=st_, stop=sp_)
                        P.i(ins, "te")
                    block.tensor(p4_ted)
                P.touch("ps2", "te"); P.touch("ps3", "te")
                P.touch("ps4", "te"); P.touch("ps5", "te")

                def p4_syr2(sync, ca=ca, cb=cb, tca=tca, tcb=tcb):
                    P.w(sync, "ve")
                    P.i(sync.dma_start(out=x_tm.ap()[:, 0, :], in_=r_dram[ca * 128:(ca + 1) * 128, :]), sq, 16)
                    P.i(sync.dma_start(out=x_tm.ap()[:, 1, :], in_=r_dram[cb * 128:(cb + 1) * 128, :]), sq, 16)
                block.sync(p4_syr2)

                def p4_vout(ve):
                    P.w(ve, "te")
                    P.w(ve, sq)
                    ve.tensor_add(x_tm.ap()[:, 0, 0:512], ps2.ap()[:], x_tm.ap()[:, 0, 0:512])
                    ve.tensor_add(x_tm.ap()[:, 0, 512:1024], ps3.ap()[:], x_tm.ap()[:, 0, 512:1024])
                    ve.tensor_add(x_tm.ap()[:, 1, 0:512], ps4.ap()[:], x_tm.ap()[:, 1, 0:512])
                    P.i(ve.tensor_add(x_tm.ap()[:, 1, 512:1024], ps5.ap()[:], x_tm.ap()[:, 1, 512:1024]), "ve")
                block.vector(p4_vout)
                P.touch("ps2", "ve"); P.touch("ps3", "ve")
                P.touch("ps4", "ve"); P.touch("ps5", "ve")

                def p4_syo(sync, ca=ca, cb=cb):
                    P.w(sync, "ve")
                    P.i(sync.dma_start(out=out_hbm[ca * 128:(ca + 1) * 128, :], in_=x_tm.ap()[:, 0, :]), sq, 16)
                    P.i(sync.dma_start(out=out_hbm[cb * 128:(cb + 1) * 128, :], in_=x_tm.ap()[:, 1, :]), sq, 16)
                block.sync(p4_syo)

        def tap_sy(sync):
            P.w(sync, "ve"); P.w(sync, "se"); P.w(sync, "ge")
            if "H_loc" in tap_t:
                P.i(sync.dma_start(out=tap_t["H_loc"], in_=H.ap()[:]), sq, 16)
            if "lgsum" in tap_t:
                P.i(sync.dma_start(out=tap_t["lgsum"], in_=lgcum.ap()[:]), sq, 16)
            if "sz" in tap_t:
                P.i(sync.dma_start(out=tap_t["sz"], in_=z_dram.rearrange("c p d -> (c p) d")), sq, 16)
            if "y_loc" in tap_t:
                P.i(sync.dma_start(out=tap_t["y_loc"], in_=y_dram.rearrange("c p d -> (c p) d")), sq, 16)
            if "resid" in tap_t:
                P.i(sync.dma_start(out=tap_t["resid"], in_=r_dram), sq, 16)
            if "negA" in tap_t:
                P.i(sync.dma_start(out=tap_t["negA"], in_=c_negA.ap()[:]), sq, 16)
            if "lam" in tap_t:
                P.i(sync.dma_start(out=tap_t["lam"], in_=lam_all.ap().rearrange("p c h -> p (c h)")), sq, 16)
        block.sync(tap_sy)

        def fin(sync):
            P.w(sync, sq)
            P.w(sync, qg)
        block.sync(fin)

    return nc


def _host_inputs(inputs):
    x = np.ascontiguousarray(inputs["L0_attn_x"])
    W_in = np.ascontiguousarray(inputs["L0_attn_W_in"])
    W_out = np.ascontiguousarray(inputs["L0_attn_W_out"])
    conv_w = np.ascontiguousarray(inputs["L0_attn_conv_w"]).reshape(XBC, 4)
    conv_b = np.ascontiguousarray(inputs["L0_attn_conv_b"])
    convw_fm = np.ascontiguousarray(conv_w.reshape(NOC, 128, 4).transpose(1, 0, 2))
    convb_fm = np.ascontiguousarray(conv_b.reshape(NOC, 128).T)
    wg = np.ascontiguousarray(inputs["L0_ffn_w_gate"].T)
    wu = np.ascontiguousarray(inputs["L0_ffn_w_up"].T)
    wd = np.ascontiguousarray(inputs["L0_ffn_w_down"].T)
    as16 = lambda a: np.ascontiguousarray(a.view(np.int16)[:, 0::2])

    ident = np.eye(128, dtype=np.float32).astype(ml_dtypes.bfloat16)
    tri = np.ascontiguousarray(np.tri(128, dtype=np.float32).T).astype(ml_dtypes.bfloat16)
    us = np.tri(128, k=-1, dtype=np.float32).astype(ml_dtypes.bfloat16)
    row = lambda a: np.ascontiguousarray(np.broadcast_to(
        np.asarray(a, dtype=np.float32).reshape(1, -1), (128, np.asarray(a).size)))

    maps = []
    for j in range(8):
        b, s = j // 4, j % 4
        seg = x[b, s * T_CORE:(s + 1) * T_CORE]
        halo = x[b, s * T_CORE - CH:s * T_CORE] if s > 0 else np.zeros((CH, D_MODEL), np.float32)
        xe = np.ascontiguousarray(np.concatenate([halo, seg], axis=0))
        cm = np.zeros((4, 4), np.float32)
        cb = np.full((4,), NEG, np.float32)
        for i in range(4):
            if i < s:
                cb[i] = 0.0
                for k in range(i + 1, s):
                    cm[i, k] = 1.0
        maps.append({
            "x_ext": xe, "w_in": W_in, "w_out": W_out,
            "convw": convw_fm, "convb": convb_fm,
            "alog": row(inputs["L0_attn_A_log"]), "dtb": row(inputs["L0_attn_dt_bias"]),
            "dpar": row(inputs["L0_attn_D_param"]), "nw": row(inputs["L0_attn_norm_w"]),
            "cm": row(cm), "cb": row(cb),
            "ident": ident, "tri": tri, "us": us,
            "wg_t": as16(wg), "wu_t": as16(wu), "wd_t": as16(wd),
        })
    return maps


def kernel(**inputs):
    maps = _host_inputs(inputs)
    nc = build_program()
    res = run_bass_kernel_spmd(nc, maps, core_ids=list(range(8)))
    out = np.empty((2, 8192, D_MODEL), dtype=np.float32)
    for j in range(8):
        b, s = j // 4, j % 4
        out[b, s * T_CORE:(s + 1) * T_CORE] = res.results[j]["out"]
    return out


# version 16
# speedup vs baseline: 1.1588x; 1.1588x over previous
import numpy as np
import ml_dtypes
import concourse.bass as bass
import concourse.mybir as mybir
from concourse.bass_utils import run_bass_kernel_spmd

F32 = mybir.dt.float32
BF16 = mybir.dt.bfloat16
I16 = mybir.dt.int16
AF = mybir.ActivationFunctionType
ALU = mybir.AluOpType
AX = mybir.AxisListType

D_MODEL = 1024
PROJ = 4256
D_INNER = 2048
XBC = 2176
HIDDEN = 3072
T_CORE = 2048
NCH = 16
CH = 128
NTL = 4
TL = 512
NOC = 17
EPS = 1e-5
NEG = -10000.0

TAPS = ()
HARD_SYNC = False
SKIP_Z = False
SKIP_DT = False
SKIP_SSD = False
SKIP_HG = False
SKIP_GX = False
SKIP_VEY = False
SKIP_TEY = False
SKIP_VEY_Y = False
SKIP_VEY_H = False
DED_ACC = False
_TAP_SPECS = {
    "xct":    ([T_CORE, D_INNER], BF16),
    "bct":    ([T_CORE, 128], BF16),
    "dt":     ([T_CORE, 32], F32),
    "sz":     ([T_CORE, D_INNER], BF16),
    "y_loc":  ([T_CORE, D_INNER], BF16),
    "H_loc":  ([64, D_INNER], F32),
    "lgsum":  ([128, 32], F32),
    "hstart": ([64, D_INNER], F32),
    "y_full": ([T_CORE, D_INNER], BF16),
    "resid":  ([T_CORE, D_MODEL], F32),
    "lam":    ([128, NCH * 32], F32),
    "xcf":    ([T_CORE, NOC * 128], BF16),
    "xpre":   ([NTL * 128, NOC * 516], BF16),
    "negA":   ([128, 32], F32),
    "S":      ([T_CORE, 96], F32),
    "alpha":  ([T_CORE, 32], F32),
    "gam":    ([T_CORE, 32], F32),
}


class Prog:
    def __init__(self, nc):
        self.nc = nc
        self.sem = {}
        self.cnt = {}
        for n in ("qs", "qg", "te", "ve", "se", "ge"):
            self.sem[n] = nc.alloc_semaphore(n)
            self.cnt[n] = 0
        self.guard = {}

    def w(self, eng, name, lag=0):
        v = self.cnt[name] - lag
        if v > 0:
            eng.wait_ge(self.sem[name], v)

    def w_val(self, eng, name, val):
        if val > 0:
            eng.wait_ge(self.sem[name], val)

    def i(self, ins, name, k=1):
        self.cnt[name] += k
        return ins.then_inc(self.sem[name], k)

    def ps(self, eng, my_eng, bank):
        g = self.guard.get(bank)
        if g is not None and g[0] != my_eng and g[1] > 0:
            eng.wait_ge(self.sem[g[0]], g[1])

    def touch(self, bank, eng_name):
        self.guard[bank] = (eng_name, self.cnt[eng_name])


def build_program(taps=None):
    global HARD_SYNC
    if taps is None:
        taps = TAPS
    nc = bass.Bass("TRN2", target_bir_lowering=False, debug=False)
    nc.has_collectives = True

    x_ext = nc.dram_tensor("x_ext", [CH + T_CORE, D_MODEL], F32, kind="ExternalInput").ap()
    w_in = nc.dram_tensor("w_in", [D_MODEL, PROJ], F32, kind="ExternalInput").ap()
    w_out = nc.dram_tensor("w_out", [D_INNER, D_MODEL], F32, kind="ExternalInput").ap()
    convw_in = nc.dram_tensor("convw", [128, NOC, 4], F32, kind="ExternalInput").ap()
    convb_in = nc.dram_tensor("convb", [128, NOC], F32, kind="ExternalInput").ap()
    alog_in = nc.dram_tensor("alog", [128, 32], F32, kind="ExternalInput").ap()
    dtb_in = nc.dram_tensor("dtb", [128, 32], F32, kind="ExternalInput").ap()
    dpar_in = nc.dram_tensor("dpar", [128, 32], F32, kind="ExternalInput").ap()
    nw_in = nc.dram_tensor("nw", [128, D_INNER], F32, kind="ExternalInput").ap()
    cm_in = nc.dram_tensor("cm", [128, 16], F32, kind="ExternalInput").ap()
    cb_in = nc.dram_tensor("cb", [128, 4], F32, kind="ExternalInput").ap()
    ident_in = nc.dram_tensor("ident", [128, 128], BF16, kind="ExternalInput").ap()
    tri_in = nc.dram_tensor("tri", [128, 128], BF16, kind="ExternalInput").ap()
    us_in = nc.dram_tensor("us", [128, 128], BF16, kind="ExternalInput").ap()
    wg_t = nc.dram_tensor("wg_t", [D_MODEL, HIDDEN], I16, kind="ExternalInput").ap()
    wu_t = nc.dram_tensor("wu_t", [D_MODEL, HIDDEN], I16, kind="ExternalInput").ap()
    wd_t = nc.dram_tensor("wd_t", [HIDDEN, D_MODEL], I16, kind="ExternalInput").ap()

    out_hbm = nc.dram_tensor("out", [T_CORE, D_MODEL], F32, kind="ExternalOutput").ap()

    z_dram = nc.dram_tensor("z_dram", [NCH, 128, D_INNER], BF16, kind="Internal").ap()
    y_dram = nc.dram_tensor("y_dram", [NCH, 128, D_INNER], BF16, kind="Internal").ap()
    r_dram = nc.dram_tensor("r_dram", [T_CORE, D_MODEL], F32, kind="Internal").ap()
    wd_bf = nc.dram_tensor("wd_bf", [HIDDEN, D_MODEL], BF16, kind="Internal").ap()
    ag_in = nc.dram_tensor("ag_in", [64, D_INNER + 32], F32, kind="Internal").ap()
    ag_out = nc.dram_tensor("ag_out", [4, 64, D_INNER + 32], F32, kind="Internal").ap()

    tap_t = {}
    for name in taps:
        shp, dt_ = _TAP_SPECS[name]
        tap_t[name] = nc.dram_tensor("tap_" + name, shp, dt_, kind="ExternalOutput").ap()

    A = nc.alloc_sbuf_tensor
    arena_w = A("arena_w", [128, 49152], BF16)
    cv_acc = A("cv_acc", [128, NOC * 128], BF16)
    arena_x = A("arena_x", [128, 17664], BF16)
    x_tm = A("x_tm", [128, 2, 1024], F32)
    xn_fm = A("xn_fm", [128, 8, TL], BF16)
    sz_sb = A("sz_sb", [128, 2, D_INNER], BF16)
    xc_tm = A("xc_tm", [128, 2, D_INNER], BF16)
    bc_tm = A("bc_tm", [128, 2, 128], BF16)
    ct_all = A("ct_all", [64, NCH, 128], BF16)
    lam_all = A("lam_all", [128, NCH, 32], F32)
    lgsnap = A("lgsnap", [128, NCH, 32], BF16)
    lgcum = A("lgcum", [128, 32], F32)
    gam_sb = A("gam_sb", [128, 2, 32], F32)
    dt_sb = A("dt_sb", [128, 2, 32], BF16)
    dtw_sb = A("dtw_sb", [128, 2, 32], F32)
    al_bf = A("al_bf", [128, 2, 32], BF16)
    dtp_sb = A("dtp_sb", [128, 32], F32)
    expd = A("expd", [128, 2, 1024], BF16)
    m_t = A("m_t", [128, 2, 1024], BF16)
    gm_t = A("gm_t", [128, 2, 128], BF16)
    bw_sb = A("bw_sb", [128, 2, 512], BF16)
    H = A("H", [64, D_INNER], F32)
    H_bf = A("H_bf", [64, D_INNER], BF16)
    y_sb = A("y_sb", [128, D_INNER], BF16)
    y2_sb = A("y2_sb", [128, D_INNER], BF16)
    ssc = A("ssc", [128, 8], F32)
    c_ident = A("c_ident", [128, 128], BF16)
    c_tri = A("c_tri", [128, 128], BF16)
    c_us = A("c_us", [128, 128], BF16)
    c_convw = A("c_convw", [128, NOC, 4], F32)
    c_convb = A("c_convb", [128, NOC], F32)
    c_negA = A("c_negA", [128, 32], F32)
    c_dtb = A("c_dtb", [128, 32], F32)
    c_D = A("c_D", [128, 32], F32)
    c_eb = A("c_eb", [128, 2], F32)

    w_in_bf = arena_w.ap()[:, 0:8 * PROJ].rearrange("p (k c) -> p k c", k=8)
    wout_bf = arena_w.ap()[:, 0:16 * 1024].rearrange("p (k c) -> p k c", k=16)
    wg_bf = arena_w.ap()[:, 0:8 * HIDDEN].rearrange("p (k c) -> p k c", k=8)
    wu_bf = arena_w.ap()[:, 8 * HIDDEN:16 * HIDDEN].rearrange("p (k c) -> p k c", k=8)
    w_stage = arena_x.ap()[:, 0:8704].bitcast(F32)
    smr = arena_x.ap()[0:1, 8704:13312].bitcast(F32)
    xbc_pre = arena_x.ap()[:, 0:NOC * 516].rearrange("p (o c) -> p o c", o=NOC)
    xbc_cv = arena_x.ap()[:, NOC * 516:NOC * 516 + NOC * 256].rearrange(
        "p (b o c) -> p b o c", b=2, o=NOC)
    agst = arena_x.ap()[:, 0:256].bitcast(F32)
    w64 = arena_x.ap()[:, 256:512].bitcast(F32)
    sslot = arena_x.ap()[:, 512:4608].bitcast(F32)
    yn_fm = arena_x.ap()[:, 8448:10496]
    yn_fm3 = yn_fm.rearrange("p (k c) -> p k c", k=16)
    rsd_pp = arena_x.ap()[:, 10496:14592].bitcast(F32).rearrange("p (b c) -> p b c", b=2)
    h_sb = arena_x.ap()[:, 0:24 * TL].rearrange("p (o c) -> p o c", o=24)
    stage16 = arena_x.ap()[:, 14592:17664].bitcast(I16)
    c_nw = arena_x.ap()[:, 0:2048]
    g_sb = arena_x.ap()[:, 12288:14336]
    c_M = arena_x.ap()[:, 4608:4640].bitcast(F32)[:, 0:16]
    gp_tmp = arena_x.ap()[:, 13568:14080]
    c_cb = arena_x.ap()[:, 4640:4648].bitcast(F32)[:, 0:4]
    w_stage2 = arena_x.ap()[:, 14592:17664].bitcast(F32)
    cvt_ve = [arena_w.ap()[:, 34048 + i * 1536:34048 + (i + 1) * 1536].rearrange("p (o c) -> p o c", o=12)
              for i in range(3)]
    cvt_gp = [arena_w.ap()[:, 38656 + i * 640:38656 + (i + 1) * 640].rearrange("p (o c) -> p o c", o=5)
              for i in range(3)]
    wdb = y_sb.ap().rearrange("p (b c) -> p b c", b=2)

    P_ = nc.alloc_psum_tensor
    ps0 = P_("ps0", [128, 512], F32)
    ps1 = P_("ps1", [128, 512], F32)
    ps2 = P_("ps2", [128, 512], F32)
    ps3 = P_("ps3", [128, 512], F32)
    ps4 = P_("ps4", [128, 512], F32)
    ps5 = P_("ps5", [128, 512], F32)
    ps6 = P_("ps6", [128, 1024], BF16)
    ps7 = P_("ps7", [128, 512], F32)

    with nc.Block() as block:
        P = Prog(nc)
        sq, qg = "qs", "qg"

        def p0_sy(sync):
            for dst, src in ((c_ident, ident_in), (c_tri, tri_in), (c_us, us_in),
                             (c_convw, convw_in), (c_convb, convb_in),
                             (c_dtb, dtb_in), (c_D, dpar_in),
                             (c_negA, alog_in)):
                P.i(sync.dma_start(out=dst.ap()[:], in_=src), sq, 16)
        block.sync(p0_sy)



        def p0_se(se):
            P.w(se, sq)
            P.i(se.activation(out=dtp_sb.ap()[:], in_=c_negA.ap()[:], func=AF.Exp), "se")
        block.scalar(p0_se)

        def p0_ve2(ve):
            P.w(ve, "se")
            ve.tensor_scalar_mul(c_negA.ap()[:], dtp_sb.ap()[:], -1.0)
            ve.memzero(H.ap()[:])
            ve.memzero(H_bf.ap()[:])
            ve.memset(c_eb.ap()[:, 0:1], EPS)
            ve.memset(c_eb.ap()[:, 1:2], 1.0)
            P.i(ve.memzero(lgcum.ap()[:]), "ve")
        block.vector(p0_ve2)

        for k in range(8):
            def p1w_sy(sync, k=k):
                P.w(sync, "ve")
                P.i(sync.dma_start(out=w_stage[:, 0:PROJ], in_=w_in[k * 128:(k + 1) * 128, :]), sq, 16)
            block.sync(p1w_sy)

            def p1w_ve(ve, k=k):
                P.w(ve, sq)
                P.i(ve.tensor_copy(w_in_bf[:, k, :], w_stage[:, 0:PROJ]), "ve")
            block.vector(p1w_ve)

        def ln_chunk(ec, xbuf, fm_col, src_is_dram_resid=None, inv_scale=1.0):
            def sy(sync):
                P.w(sync, "ve")
                P.w(sync, "se")
                src = x_ext[ec * 128:(ec + 1) * 128, :] if src_is_dram_resid is None \
                    else r_dram[src_is_dram_resid * 128:(src_is_dram_resid + 1) * 128, :]
                P.i(sync.dma_start(out=x_tm.ap()[:, xbuf, :], in_=src), sq, 16)
            block.sync(sy)

            def se1(se):
                P.w(se, sq)
                P.w(se, "te")
                P.w(se, "ve")
                P.i(se.activation(out=cv_acc.ap()[:, 0:1024], in_=x_tm.ap()[:, xbuf, :], func=AF.Square,
                                  accum_out=ssc.ap()[:, 1:2]), "se")
            block.scalar(se1)

            def ve1(ve):
                P.w(ve, sq)
                P.i(ve.reduce_sum(ssc.ap()[:, 0:1], x_tm.ap()[:, xbuf, :], axis=AX.X), "ve")
                P.w(ve, "se")
                ve.tensor_scalar_mul(ssc.ap()[:, 2:3], ssc.ap()[:, 0:1], 1.0 / D_MODEL)
                ve.tensor_scalar_mul(ssc.ap()[:, 3:4], ssc.ap()[:, 1:2], 1.0 / D_MODEL)
                P.i(ve.scalar_tensor_tensor(ssc.ap()[:, 4:5], ssc.ap()[:, 2:3], ssc.ap()[:, 2:3],
                                            ssc.ap()[:, 3:4], op0=ALU.mult, op1=ALU.subtract), "ve")
            block.vector(ve1)

            def se2(se):
                P.w(se, "ve")
                se.activation(out=ssc.ap()[:, 5:6], in_=ssc.ap()[:, 4:5], func=AF.Ln, bias=c_eb.ap()[:, 0:1], scale=-1.0)
                P.i(se.activation(out=ssc.ap()[:, 6:7], in_=ssc.ap()[:, 5:6], func=AF.Exp, scale=-0.5), "se")
            block.scalar(se2)

            def ve2(ve):
                P.w(ve, "se")
                rst = ssc.ap()[:, 6:7]
                if inv_scale != 1.0:
                    ve.tensor_scalar_mul(ssc.ap()[:, 7:8], ssc.ap()[:, 6:7], inv_scale)
                    rst = ssc.ap()[:, 7:8]
                P.i(ve.tensor_scalar(cv_acc.ap()[:, 0:1024], x_tm.ap()[:, xbuf, :], ssc.ap()[:, 2:3],
                                     rst, op0=ALU.subtract, op1=ALU.mult), "ve")
            block.vector(ve2)

            def te1(te):
                P.w(te, "ve")
                P.ps(te, "te", "ps6")
                for k in range(8):
                    ins = te.transpose(ps6.ap()[:, k * 128:(k + 1) * 128],
                                       cv_acc.ap()[:, k * 128:(k + 1) * 128], c_ident.ap()[:])
                P.i(ins, "te")
            block.tensor(te1)
            P.touch("ps6", "te")

            def ve3(ve):
                P.w(ve, "te")
                P.i(ve.tensor_copy(xn_fm.ap()[:, :, fm_col:fm_col + 128],
                                   ps6.ap().rearrange("p (k c) -> p k c", k=8)), "ve")
            block.vector(ve3)
            P.touch("ps6", "ve")

        def xbc_mm(a, b, halo_only=False):
            for oc in range(NOC):
                bank = "ps0" if oc % 2 == 0 else "ps1"
                psx = ps0 if oc % 2 == 0 else ps1

                def temm(te, oc=oc, psx=psx, bank=bank):
                    P.ps(te, "te", bank)
                    P.w(te, "ve")
                    for k in range(8):
                        ins = te.matmul(psx.ap()[:, 0:b - a],
                                        w_in_bf[:, k, D_INNER + oc * 128:D_INNER + (oc + 1) * 128],
                                        xn_fm.ap()[:, k, a:b], start=(k == 0), stop=(k == 7))
                    P.i(ins, "te")
                block.tensor(temm)
                P.touch(bank, "te")

                def seev(ve, oc=oc, psx=psx, bank=bank):
                    P.w(ve, "te")
                    if halo_only:
                        P.i(ve.tensor_copy(xbc_pre[:, oc, 0:4], psx.ap()[:, b - a - 4:b - a]), "ve")
                    else:
                        P.i(ve.tensor_copy(xbc_pre[:, oc, a + 4:b + 4], psx.ap()[:, 0:b - a]), "ve")
                block.vector(seev)
                P.touch(bank, "ve")

        ln_chunk(0, 0, 0)
        xbc_mm(0, 128, halo_only=True)

        frm_se = {}
        fz_gz = {}
        frm_ve = {}
        frm_dec = {}
        bk_vve = {}
        bk_gy = {}
        fcv_ve = {}
        fcv_gp = {}
        ftr_te = {}
        mcnt = {}

        def emit_frontend_a(c):
            tile, tc = c // 4, c % 4
            p = c % 2
            cvb = c % 2
            if tc == 0:
                for q in range(4):
                    ln_chunk(1 + tile * 4 + q, q % 2, q * 128)
                xbc_mm(0, TL)

            ztec = {}
            for zpair in range(2):
                for zc in (zpair * 2, zpair * 2 + 1):
                    bank = "ps2" if zc % 2 == 0 else "ps3"
                    psz = ps2 if zc % 2 == 0 else ps3

                    def tez(te, tc=tc, zc=zc, psz=psz, bank=bank):
                        P.ps(te, "te", bank)
                        for k in range(8):
                            ins = te.matmul(psz.ap()[:], xn_fm.ap()[:, k, tc * 128:(tc + 1) * 128],
                                            w_in_bf[:, k, zc * 512:(zc + 1) * 512],
                                            start=(k == 0), stop=(k == 7))
                        P.i(ins, "te")
                        ztec[zc] = P.cnt["te"]
                    block.tensor(tez)
                    P.touch(bank, "te")

                def vzev(ve, p=p, c=c, zpair=zpair):
                    for zc in (zpair * 2, zpair * 2 + 1):
                        psz = ps2 if zc % 2 == 0 else ps3
                        P.w_val(ve, "te", ztec[zc])
                        if zc == 0 and c >= 1:
                            P.w_val(ve, qg, fz_gz.get(c - 1, 0))
                        ins = ve.tensor_copy(sz_sb.ap()[:, 1 - p, zc * 512:(zc + 1) * 512], psz.ap()[:])
                    P.i(ins, "ve")
                block.vector(vzev)
                P.touch("ps2", "ve")
                P.touch("ps3", "ve")

            def tedt(te, tc=tc):
                P.ps(te, "te", "ps5")
                for k in range(8):
                    ins = te.matmul(ps5.ap()[:, 96:128], xn_fm.ap()[:, k, tc * 128:(tc + 1) * 128],
                                    w_in_bf[:, k, 4224:4256], start=(k == 0), stop=(k == 7))
                P.i(ins, "te")
            block.tensor(tedt)
            P.touch("ps5", "te")

            def vdt(ve):
                P.w(ve, "te")
                P.i(ve.tensor_add(dtp_sb.ap()[:], ps5.ap()[:, 96:128], c_dtb.ap()[:]), "ve")
            block.vector(vdt)
            P.touch("ps5", "ve")

            def sdt(se, p=p):
                P.w(se, "ve")
                se.activation(out=dtp_sb.ap()[:], in_=dtp_sb.ap()[:], func=AF.Exp)
                P.i(se.activation(out=dt_sb.ap()[:, p, :], in_=dtp_sb.ap()[:], func=AF.Ln,
                                  bias=c_eb.ap()[:, 1:2]), "se")
            block.scalar(sdt)

            def valpha(ve, p=p, c=c):
                P.w(ve, "se")
                if c >= 2:
                    P.w_val(ve, "te", frm_dec.get(c - 2, 0))
                P.i(ve.tensor_tensor(al_bf.ap()[:, p, :], dt_sb.ap()[:, p, :], c_negA.ap()[:],
                                     op=ALU.mult), "ve")
            block.vector(valpha)

            def tedec(te, p=p):
                P.ps(te, "te", "ps5")
                P.w(te, "ve")
                te.matmul(ps5.ap()[:, 0:32], c_tri.ap()[:], al_bf.ap()[:, p, :], start=True, stop=True)
                P.i(te.matmul(ps5.ap()[:, 32:64], c_us.ap()[:], al_bf.ap()[:, p, :], start=True, stop=True), "te")
                frm_dec[c] = P.cnt["te"]
            block.tensor(tedec)
            P.touch("ps5", "te")

            def sdec(se, c=c):
                P.w(se, "te")
                se.activation(out=lam_all.ap()[:, c, :], in_=ps5.ap()[:, 0:32], func=AF.Exp)
                P.i(se.activation(out=dtp_sb.ap()[:], in_=ps5.ap()[:, 32:64], func=AF.Exp), "se")
            block.scalar(sdec)

            def vdec(ve, c=c, p=p):
                P.w(ve, "se")
                ve.tensor_copy(lgsnap.ap()[:, c, :], lgcum.ap()[:])
                ve.tensor_add(lgcum.ap()[:], lgcum.ap()[:], ps5.ap()[:, 0:32])
                ve.tensor_add(lgcum.ap()[:], lgcum.ap()[:], ps5.ap()[:, 32:64])
                ve.tensor_mul(gam_sb.ap()[:, p, :], lam_all.ap()[:, c, :], dtp_sb.ap()[:])
                P.i(ve.tensor_mul(dtw_sb.ap()[:, p, :], dt_sb.ap()[:, p, :], dtp_sb.ap()[:]), "ve")
            block.vector(vdec)
            P.touch("ps5", "ve")

            NOC_VE = 12
            acc3 = cv_acc.ap().rearrange("p (o c) -> p o c", o=NOC)

            def vconv_ve(ve, tc=tc, p=p, c=c):
                P.w(ve, "se")
                base = tc * 128
                sl = slice(0, NOC_VE)
                nb = [128, NOC_VE, 128]

                def prod(eng, dst, kk):
                    eng.tensor_tensor(dst, xbc_pre[:, sl, base + 1 + kk:base + 129 + kk],
                                      c_convw.ap()[:, sl, kk:kk + 1].broadcast_to(nb), op=ALU.mult)
                prod(ve, acc3[:, sl, :], 0)
                prod(ve, cvt_ve[0], 1)
                prod(ve, cvt_ve[1], 2)
                ve.tensor_tensor(acc3[:, sl, :], acc3[:, sl, :], cvt_ve[0], op=ALU.add)
                prod(ve, cvt_ve[2], 3)
                ve.tensor_tensor(acc3[:, sl, :], acc3[:, sl, :], cvt_ve[1], op=ALU.add)
                P.i(ve.tensor_tensor(acc3[:, sl, :], acc3[:, sl, :], cvt_ve[2], op=ALU.add), "ve")
                fcv_ve[c] = P.cnt["ve"]
            block.vector(vconv_ve)

            def vconv_gp(g, tc=tc, c=c):
                P.w(g, "se")
                P.w(g, "te")
                P.w(g, "ve")
                base = tc * 128
                sl = slice(NOC_VE, NOC)
                nb = [128, NOC - NOC_VE, 128]

                def prod(eng, dst, kk):
                    eng.tensor_tensor(dst, xbc_pre[:, sl, base + 1 + kk:base + 129 + kk],
                                      c_convw.ap()[:, sl, kk:kk + 1].broadcast_to(nb), op=ALU.mult)
                prod(g, acc3[:, sl, :], 0)
                prod(g, cvt_gp[0], 1)
                prod(g, cvt_gp[1], 2)
                g.tensor_tensor(acc3[:, sl, :], acc3[:, sl, :], cvt_gp[0], op=ALU.add)
                prod(g, cvt_gp[2], 3)
                g.tensor_tensor(acc3[:, sl, :], acc3[:, sl, :], cvt_gp[1], op=ALU.add)
                P.i(g.tensor_tensor(acc3[:, sl, :], acc3[:, sl, :], cvt_gp[2], op=ALU.add), "ge")
                fcv_gp[c] = P.cnt["ge"]
            block.gpsimd(vconv_gp)

        def emit_frontend_b(c):
            tile, tc = c // 4, c % 4
            p = c % 2
            cvb = c % 2

            def sconv(se, cvb=cvb, p=p, c=c):
                P.w_val(se, "ge", fcv_gp[c])
                P.w_val(se, "ve", fcv_ve[c])
                P.w_val(se, "te", ftr_te.get(c - 2, 0))
                if c >= 2:
                    P.w_val(se, qg, bk_gy.get(c - 2, 0))
                se.activation(out=sz_sb.ap()[:, p, :], in_=sz_sb.ap()[:, 1 - p, :], func=AF.Silu)
                for oc in range(NOC):
                    ins = se.activation(out=xbc_cv[:, cvb, oc, :], in_=cv_acc.ap()[:, oc * 128:(oc + 1) * 128],
                                        func=AF.Silu, bias=c_convb.ap()[:, oc:oc + 1])
                P.i(ins, "se")
            block.scalar(sconv)

            def gz(g, c=c, p=p):
                P.w(g, "se")
                P.i(g.dma_start(out=z_dram[c], in_=sz_sb.ap()[:, p, :]), qg, 16)
                fz_gz[c] = P.cnt[qg]
            block.gpsimd(gz)

            for half in range(2):
                def tetr(te, half=half, cvb=cvb):
                    P.w(te, "se")
                    P.ps(te, "te", "ps6")
                    for q in range(8):
                        ins = te.transpose(ps6.ap()[:, q * 128:(q + 1) * 128],
                                           xbc_cv[:, cvb, half * 8 + q, :], c_ident.ap()[:])
                    P.i(ins, "te")
                block.tensor(tetr)
                P.touch("ps6", "te")

                def setr(ve, half=half, p=p):
                    P.w(ve, "te")
                    P.w(ve, "ge")
                    P.i(ve.tensor_copy(xc_tm.ap()[:, p, half * 1024:(half + 1) * 1024], ps6.ap()[:]), "ve")
                block.vector(setr)
                P.touch("ps6", "ve")

            def tebc(te, cvb=cvb):
                P.w(te, "se")
                P.ps(te, "te", "ps6")
                P.i(te.transpose(ps6.ap()[:, 0:128], xbc_cv[:, cvb, 16, :], c_ident.ap()[:]), "te")
            block.tensor(tebc)
            P.touch("ps6", "te")

            def sebc(ve, p=p):
                P.w(ve, "te")
                P.i(ve.tensor_copy(bc_tm.ap()[:, p, :], ps6.ap()[:, 0:128]), "ve")
            block.vector(sebc)
            P.touch("ps6", "ve")

            def tect(te, p=p):
                P.w(te, "ve")
                P.ps(te, "te", "ps6")
                P.i(te.transpose(ps6.ap()[0:64, 128:256], bc_tm.ap()[:, p, 64:128], c_ident.ap()[:]), "te")
            block.tensor(tect)
            P.touch("ps6", "te")

            def sect(ve, c=c):
                P.w(ve, "te")
                P.i(ve.tensor_copy(ct_all.ap()[:, c, :], ps6.ap()[0:64, 128:256]), "ve")
            block.vector(sect)
            P.touch("ps6", "ve")

            def teg(te, c=c, cvb=cvb):
                P.ps(te, "te", "ps5")
                P.w(te, "se")
                P.w(te, "ve")
                P.i(te.matmul(ps5.ap()[:, 128:256], xbc_cv[0:64, cvb, 16, :],
                              ct_all.ap()[:, c, :], start=True, stop=True), "te")
                ftr_te[c] = P.cnt["te"]
            block.tensor(teg)
            P.touch("ps5", "te")

            def veg(ve, p=p):
                P.w(ve, "te")
                P.i(ve.tensor_tensor(gm_t.ap()[:, p, :], ps5.ap()[:, 128:256], c_tri.ap()[:], op=ALU.mult), "ve")
            block.vector(veg)
            P.touch("ps5", "ve")
            frm_se[c] = P.cnt["se"]
            frm_ve[c] = P.cnt["ve"]

            if tc == 3 and tile < NTL - 1:
                def vhalo(ve):
                    P.w(ve, "se")
                    P.i(ve.tensor_copy(xbc_pre[:, :, 0:4], xbc_pre[:, :, 512:516]), "ve")
                block.vector(vhalo)

        def emit_backend(c):
            p = c % 2

            def em_W(hg):
                def f(ve, hg=hg):
                    b = hg % 2
                    ve.tensor_tensor(
                        expd.ap()[:, b, :].rearrange("p (h t) -> p h t", h=8),
                        al_bf.ap()[:, p, hg * 8:(hg + 1) * 8].unsqueeze(2).broadcast_to([128, 8, 128]),
                        c_tri.ap().unsqueeze(1).broadcast_to([128, 8, 128]),
                        op=ALU.mult)
                    P.i(ve.tensor_tensor(
                        bw_sb.ap()[:, b, :].rearrange("p (h n) -> p h n", h=8),
                        bc_tm.ap()[:, p, 0:64].unsqueeze(1).broadcast_to([128, 8, 64]),
                        dtw_sb.ap()[:, p, hg * 8:(hg + 1) * 8].unsqueeze(2).broadcast_to([128, 8, 64]),
                        op=ALU.mult), "ve")
                    mcnt[("W", hg)] = P.cnt["ve"]
                block.vector(f)

            def em_D(hg):
                def f(te, hg=hg):
                    b = hg % 2
                    P.w_val(te, "ve", mcnt[("W", hg)])
                    P.ps(te, "te", "ps7")
                    te.matmul(ps7.ap()[:], c_us.ap()[:], expd.ap()[:, b, 0:512], start=True, stop=True)
                    P.ps(te, "te", "ps4")
                    P.i(te.matmul(ps4.ap()[:], c_us.ap()[:], expd.ap()[:, b, 512:1024],
                                  start=True, stop=True), "te")
                    mcnt[("D", hg)] = P.cnt["te"]
                block.tensor(f)
                P.touch("ps7", "te"); P.touch("ps4", "te")

            def em_E(hg):
                def f(se, hg=hg):
                    b = hg % 2
                    P.w_val(se, "te", mcnt[("D", hg)])
                    se.activation(out=expd.ap()[:, b, 0:512], in_=ps7.ap()[:], func=AF.Exp)
                    P.i(se.activation(out=expd.ap()[:, b, 512:1024], in_=ps4.ap()[:], func=AF.Exp), "se")
                    mcnt[("E", hg)] = P.cnt["se"]
                block.scalar(f)
                P.touch("ps7", "se"); P.touch("ps4", "se")

            def em_M(hg):
                def f(ve, hg=hg):
                    b = hg % 2
                    P.w_val(ve, "se", mcnt[("E", hg)])
                    for q in range(8):
                        ins = ve.scalar_tensor_tensor(
                            m_t.ap()[:, b, q * 128:(q + 1) * 128],
                            expd.ap()[:, b, q * 128:(q + 1) * 128],
                            dt_sb.ap()[:, p, hg * 8 + q:hg * 8 + q + 1],
                            gm_t.ap()[:, p, :], op0=ALU.mult, op1=ALU.mult)
                    P.i(ins, "ve")
                    mcnt[("M", hg)] = P.cnt["ve"]
                block.vector(f)

            def em_Y(hg):
                def f(te, hg=hg):
                    b = hg % 2
                    P.w_val(te, "ve", mcnt[("M", hg)])
                    if hg == 0:
                        P.w_val(te, "se", frm_se[c])
                        P.w_val(te, "ve", frm_ve[c])
                    if c >= 1:
                        P.w_val(te, "ve", bk_vve.get((c - 1, hg), 0))
                    P.ps(te, "te", "ps0")
                    P.ps(te, "te", "ps1")
                    P.ps(te, "te", "ps4")
                    for q in range(8):
                        h = hg * 8 + q
                        te.matmul(ps0.ap()[:, q * 64:(q + 1) * 64], m_t.ap()[:, b, q * 128:(q + 1) * 128],
                                  xc_tm.ap()[:, p, h * 64:(h + 1) * 64], start=True, stop=True)
                    for q in range(8):
                        h = hg * 8 + q
                        te.matmul(ps1.ap()[:, q * 64:(q + 1) * 64], ct_all.ap()[:, c, :],
                                  H_bf.ap()[:, h * 64:(h + 1) * 64], start=True, stop=True)
                    for q in range(8):
                        h = hg * 8 + q
                        ins = te.matmul(ps4.ap()[0:64, q * 64:(q + 1) * 64],
                                        bw_sb.ap()[:, hg % 2, q * 64:(q + 1) * 64],
                                        xc_tm.ap()[:, p, h * 64:(h + 1) * 64], start=True, stop=True)
                    P.i(ins, "te")
                    mcnt[("Y", hg)] = P.cnt["te"]
                block.tensor(f)
                P.touch("ps0", "te"); P.touch("ps1", "te"); P.touch("ps4", "te")

            def em_Vve(hg):
                def f(ve, hg=hg):
                    P.w_val(ve, "te", mcnt[("Y", hg)])
                    if hg == 0 and c >= 1:
                        P.w_val(ve, qg, bk_gy.get(c - 1, 0))
                    sl = slice(hg * 512, (hg + 1) * 512)
                    ve.tensor_tensor(y2_sb.ap()[:, sl].rearrange("p (h d) -> p h d", h=8),
                                     ps1.ap().rearrange("p (h d) -> p h d", h=8),
                                     lam_all.ap()[:, c, hg * 8:(hg + 1) * 8].unsqueeze(2).broadcast_to([128, 8, 64]),
                                     op=ALU.mult)
                    P.i(ve.tensor_add(y_sb.ap()[:, sl], y2_sb.ap()[:, sl], ps0.ap()[:]), "ve")
                    mcnt[("Vy", hg)] = P.cnt["ve"]
                    ve.tensor_tensor(H.ap()[:, sl].rearrange("p (h d) -> p h d", h=8),
                                     H.ap()[:, sl].rearrange("p (h d) -> p h d", h=8),
                                     gam_sb.ap()[0:64, p, hg * 8:(hg + 1) * 8].unsqueeze(2).broadcast_to([64, 8, 64]),
                                     op=ALU.mult)
                    ve.tensor_add(H.ap()[:, sl], H.ap()[:, sl], ps4.ap()[0:64, :])
                    P.i(ve.tensor_copy(H_bf.ap()[:, sl], H.ap()[:, sl]), "ve")
                    bk_vve[(c, hg)] = P.cnt["ve"]
                block.vector(f)
                P.touch("ps0", "ve"); P.touch("ps1", "ve"); P.touch("ps4", "ve")

            def em_Vgp(hg):
                def f(g, hg=hg):
                    sl = slice(hg * 512, (hg + 1) * 512)
                    P.w_val(g, "ve", mcnt[("Vy", hg)])
                    g.tensor_tensor(gp_tmp.rearrange("p (h d) -> p h d", h=8),
                                    xc_tm.ap()[:, p, sl].rearrange("p (h d) -> p h d", h=8),
                                    c_D.ap()[:, hg * 8:(hg + 1) * 8].unsqueeze(2).broadcast_to([128, 8, 64]),
                                    op=ALU.mult)
                    P.i(g.tensor_tensor(y_sb.ap()[:, sl], y_sb.ap()[:, sl], gp_tmp, op=ALU.add), "ge")
                    mcnt[("Vgp", hg)] = P.cnt["ge"]
                block.gpsimd(f)

            em_W(0); em_D(0); em_E(0)
            for hg in range(4):
                em_M(hg)
                if hg + 1 < 4:
                    em_W(hg + 1)
                em_Y(hg)
                em_Vve(hg)
                if hg + 1 < 4:
                    em_D(hg + 1)
                    em_E(hg + 1)
                em_Vgp(hg)

            def gy(g, c=c):
                P.w_val(g, "ve", bk_vve[(c, 3)])
                P.i(g.dma_start(out=y_dram[c], in_=y_sb.ap()[:]), qg, 16)
                bk_gy[c] = P.cnt[qg]
            block.gpsimd(gy)

        for c in range(NCH):
            emit_frontend_a(c)
            if c >= 1:
                emit_backend(c - 1)
            emit_frontend_b(c)
        emit_backend(NCH - 1)

        for k in range(16):
            def p3w_sy(sync, k=k):
                P.w(sync, "ve")
                if k == 0:
                    P.w(sync, "te")
                P.i(sync.dma_start(out=w_stage2[:, 0:1024], in_=w_out[k * 128:(k + 1) * 128, :]), sq, 16)
            block.sync(p3w_sy)

            def p3w_ve(ve, k=k):
                P.w(ve, sq)
                P.i(ve.tensor_copy(wout_bf[:, k, :], w_stage2[:, 0:1024]), "ve")
            block.vector(p3w_ve)

        def p2_sy(sync):
            P.w(sync, "ve")
            P.i(sync.dma_start(out=ag_in[:, 0:D_INNER], in_=H.ap()[:]), sq, 16)
            P.i(sync.dma_start(out=ag_in[:, D_INNER:D_INNER + 32], in_=lgcum.ap()[0:64, :]), sq, 16)
        block.sync(p2_sy)

        def p2_cc(g):
            P.w(g, sq)
            P.w(g, qg)
            P.i(g.collective_compute("AllGather", ALU.bypass, ins=[ag_in], outs=[ag_out],
                                     replica_groups=[[0, 1, 2, 3], [4, 5, 6, 7]]), "ge")
        block.gpsimd(p2_cc)

        def p2_sy2(sync):
            P.w(sync, "ge")
            P.i(sync.dma_start(out=agst[0:64, 0:128].rearrange("p (i h) -> p i h", i=4),
                               in_=ag_out[:, :, D_INNER:D_INNER + 32].rearrange("i p h -> p i h")), sq, 16)
            P.i(sync.dma_start(out=c_M, in_=cm_in), sq, 16)
            P.i(sync.dma_start(out=c_cb, in_=cb_in), sq, 16)
        block.sync(p2_sy2)

        def p2_ve(ve):
            P.w(ve, sq)
            for i in range(4):
                ve.tensor_scalar_mul(w64[0:64, i * 32:(i + 1) * 32], agst[0:64, 0:32],
                                     c_M[0:64, i * 4:i * 4 + 1])
                for k in range(1, 4):
                    ve.scalar_tensor_tensor(w64[0:64, i * 32:(i + 1) * 32], agst[0:64, k * 32:(k + 1) * 32],
                                            c_M[0:64, i * 4 + k:i * 4 + k + 1],
                                            w64[0:64, i * 32:(i + 1) * 32], op0=ALU.mult, op1=ALU.add)
                ve.tensor_scalar_add(w64[0:64, i * 32:(i + 1) * 32], w64[0:64, i * 32:(i + 1) * 32],
                                     c_cb[0:64, i:i + 1])
            P.i(ve.engine_nop(), "ve")
        block.vector(p2_ve)

        def p2_se(se):
            P.w(se, "ve")
            P.i(se.activation(out=agst[0:64, 0:128], in_=w64[0:64, :], func=AF.Exp), "se")
        block.scalar(p2_se)

        def p2_vz(ve):
            P.w(ve, "se")
            P.i(ve.memzero(H.ap()[:]), "ve")
        block.vector(p2_vz)

        for i in range(4):
            def p2_syl(sync, i=i):
                P.w(sync, "ve")
                P.i(sync.dma_start(out=sslot[0:64, 0:D_INNER], in_=ag_out[i, :, 0:D_INNER]), sq, 16)
            block.sync(p2_syl)

            def p2_vel(ve, i=i):
                P.w(ve, sq)
                ve.tensor_tensor(sslot[0:64, 0:D_INNER].rearrange("p (h d) -> p h d", h=32),
                                 sslot[0:64, 0:D_INNER].rearrange("p (h d) -> p h d", h=32),
                                 agst[0:64, i * 32:(i + 1) * 32].unsqueeze(2).broadcast_to([64, 32, 64]),
                                 op=ALU.mult)
                P.i(ve.tensor_add(H.ap()[:], H.ap()[:], sslot[0:64, 0:D_INNER]), "ve")
            block.vector(p2_vel)

        def p2_vbf(ve):
            P.i(ve.tensor_copy(H_bf.ap()[:], H.ap()[:]), "ve")
        block.vector(p2_vbf)

        if "hstart" in tap_t:
            def thstart(sync):
                P.w(sync, "ve")
                P.i(sync.dma_start(out=tap_t["hstart"], in_=H.ap()[:]), sq, 16)
            block.sync(thstart)



        nw_done = [0]

        def p3_nw(g):
            P.w(g, "ve")
            P.i(g.dma_start(out=c_nw, in_=nw_in), qg, 16)
            nw_done[0] = P.cnt[qg]
        block.gpsimd(p3_nw)

        for k in range(24):
            def p3wd(g, k=k):
                if k == 0:
                    P.w(g, "ve")
                P.i(g.dma_start(out=stage16[:, 0:1024], in_=wd_t[k * 128:(k + 1) * 128, :]), qg, 16)
                P.w(g, qg)
                for hf in range(2):
                    P.i(g.tensor_copy(cv_acc.ap()[:, 0:512], stage16[:, hf * 512:(hf + 1) * 512]), "ge")
                    P.i(g.dma_start(out=wd_bf[k * 128:(k + 1) * 128, hf * 512:(hf + 1) * 512],
                                    in_=cv_acc.ap()[:, 0:512]), qg, 16)
                    P.w(g, qg)
            block.gpsimd(p3wd)

        for c in range(NCH):
            yb = c % 2

            def p3_syc(sync, c=c, yb=yb):
                P.w(sync, "ve")
                P.w(sync, "te")
                P.i(sync.dma_start(out=y_sb.ap()[:], in_=y_dram[c]), sq, 16)
                P.i(sync.dma_start(out=sz_sb.ap()[:, yb, :], in_=z_dram[c]), sq, 16)
                P.i(sync.dma_start(out=x_tm.ap()[:, yb, :], in_=x_ext[(c + 1) * 128:(c + 2) * 128, :]), sq, 16)
            block.sync(p3_syc)

            def p3_slam(se, c=c):
                P.w(se, "ve")
                P.i(se.activation(out=dtp_sb.ap()[:], in_=lgsnap.ap()[:, c, :], func=AF.Exp), "se")
            block.scalar(p3_slam)

            def p3_vlam(ve, c=c):
                P.w(ve, "se")
                P.i(ve.tensor_mul(gam_sb.ap()[:, 0, :], lam_all.ap()[:, c, :], dtp_sb.ap()[:]), "ve")
            block.vector(p3_vlam)

            for hg in range(4):
                def p3_tec(te, c=c, hg=hg):
                    P.ps(te, "te", "ps1")
                    if hg == 0:
                        P.w(te, "ve")
                    for q in range(8):
                        h = hg * 8 + q
                        ins = te.matmul(ps1.ap()[:, q * 64:(q + 1) * 64], ct_all.ap()[:, c, :],
                                        H_bf.ap()[:, h * 64:(h + 1) * 64], start=True, stop=True)
                    P.i(ins, "te")
                block.tensor(p3_tec)
                P.touch("ps1", "te")

                def p3_vec(ve, c=c, hg=hg):
                    P.w(ve, "te")
                    P.w(ve, sq)
                    sl = slice(hg * 512, (hg + 1) * 512)
                    ve.tensor_tensor(y2_sb.ap()[:, sl].rearrange("p (h d) -> p h d", h=8),
                                     ps1.ap().rearrange("p (h d) -> p h d", h=8),
                                     gam_sb.ap()[:, 0, hg * 8:(hg + 1) * 8].unsqueeze(2).broadcast_to([128, 8, 64]),
                                     op=ALU.mult)
                    P.i(ve.tensor_add(y_sb.ap()[:, sl], y_sb.ap()[:, sl], y2_sb.ap()[:, sl]), "ve")
                block.vector(p3_vec)
                P.touch("ps1", "ve")

            if "y_full" in tap_t:
                def tyf(sync, c=c):
                    P.w(sync, "ve")
                    P.i(sync.dma_start(out=tap_t["y_full"][c * 128:(c + 1) * 128, :], in_=y_sb.ap()[:]), sq, 16)
                block.sync(tyf)

            def p3_vg(ve, yb=yb):
                P.i(ve.tensor_mul(y_sb.ap()[:], y_sb.ap()[:], sz_sb.ap()[:, yb, :]), "ve")
            block.vector(p3_vg)

            def p3_srms(se):
                P.w(se, "ve")
                se.activation(out=y2_sb.ap()[:, 0:1024], in_=y_sb.ap()[:, 0:1024], func=AF.Square,
                              accum_out=ssc.ap()[:, 0:1])
                P.i(se.activation(out=y2_sb.ap()[:, 1024:2048], in_=y_sb.ap()[:, 1024:2048], func=AF.Square,
                                  accum_out=ssc.ap()[:, 1:2]), "se")
            block.scalar(p3_srms)

            def p3_vrms(ve):
                P.w(ve, "se")
                P.i(ve.tensor_add(ssc.ap()[:, 2:3], ssc.ap()[:, 0:1], ssc.ap()[:, 1:2]), "ve")
            block.vector(p3_vrms)

            def p3_srstd(se):
                P.w(se, "ve")
                se.activation(out=ssc.ap()[:, 3:4], in_=ssc.ap()[:, 2:3], func=AF.Ln,
                              bias=c_eb.ap()[:, 0:1], scale=1.0 / D_INNER)
                P.i(se.activation(out=ssc.ap()[:, 4:5], in_=ssc.ap()[:, 3:4], func=AF.Exp, scale=-0.5), "se")
            block.scalar(p3_srstd)

            def p3_vyn(ve, c=c):
                P.w(ve, "se")
                if c == 0:
                    P.w_val(ve, qg, nw_done[0])
                P.i(ve.scalar_tensor_tensor(y_sb.ap()[:], y_sb.ap()[:], ssc.ap()[:, 4:5], c_nw,
                                            op0=ALU.mult, op1=ALU.mult), "ve")
            block.vector(p3_vyn)

            for half in range(2):
                def p3_tet(te, half=half):
                    P.w(te, "ve")
                    P.ps(te, "te", "ps6")
                    for q in range(8):
                        ins = te.transpose(ps6.ap()[:, q * 128:(q + 1) * 128],
                                           y_sb.ap()[:, (half * 8 + q) * 128:(half * 8 + q + 1) * 128],
                                           c_ident.ap()[:])
                    P.i(ins, "te")
                block.tensor(p3_tet)
                P.touch("ps6", "te")

                def p3_set(ve, half=half):
                    P.w(ve, "te")
                    P.i(ve.tensor_copy(yn_fm[:, half * 1024:(half + 1) * 1024], ps6.ap()[:]), "ve")
                block.vector(p3_set)
                P.touch("ps6", "ve")

            def p3_teo(te):
                P.w(te, "ve")
                P.ps(te, "te", "ps2")
                P.ps(te, "te", "ps3")
                for k in range(16):
                    te.matmul(ps2.ap()[:], yn_fm3[:, k, :], wout_bf[:, k, 0:512], start=(k == 0), stop=(k == 15))
                for k in range(16):
                    ins = te.matmul(ps3.ap()[:], yn_fm3[:, k, :], wout_bf[:, k, 512:1024],
                                    start=(k == 0), stop=(k == 15))
                P.i(ins, "te")
            block.tensor(p3_teo)
            P.touch("ps2", "te"); P.touch("ps3", "te")

            def p3_vres(ve, yb=yb):
                P.w(ve, "te")
                ve.tensor_add(rsd_pp[:, yb, 0:512], ps2.ap()[:], x_tm.ap()[:, yb, 0:512])
                P.i(ve.tensor_add(rsd_pp[:, yb, 512:1024], ps3.ap()[:], x_tm.ap()[:, yb, 512:1024]), "ve")
            block.vector(p3_vres)
            P.touch("ps2", "ve"); P.touch("ps3", "ve")

            def p3_syr(sync, c=c, yb=yb):
                P.w(sync, "ve")
                P.i(sync.dma_start(out=r_dram[c * 128:(c + 1) * 128, :], in_=rsd_pp[:, yb, :]), sq, 16)
            block.sync(p3_syr)

        for wsrc, wdst in ((wg_t, wg_bf), (wu_t, wu_bf)):
            for k in range(8):
                def p4w_sy(sync, wsrc=wsrc, k=k):
                    P.w(sync, "ve")
                    P.w(sync, "ge")
                    P.i(sync.dma_start(out=stage16[:, 0:HIDDEN], in_=wsrc[k * 128:(k + 1) * 128, :]), sq, 16)
                block.sync(p4w_sy)

                def p4w_ve(ve, wdst=wdst, k=k):
                    P.w(ve, sq)
                    P.i(ve.tensor_copy(wdst[:, k, :], stage16[:, 0:HIDDEN]), "ve")
                block.vector(p4w_ve)

        first_down = [True]
        for tile in range(NTL):
            for tc in range(4):
                c = tile * 4 + tc
                ln_chunk(0, tc % 2, tc * 128, src_is_dram_resid=c, inv_scale=1.0 / 127.0)

            for hc in range(24):
                gb, gbn = (ps0, "ps0") if hc % 2 == 0 else (ps4, "ps4")
                ub, ubn = (ps1, "ps1") if hc % 2 == 0 else (ps5, "ps5")

                def p4_teg(te, hc=hc, gb=gb, ub=ub, gbn=gbn, ubn=ubn):
                    if hc == 0:
                        P.w(te, "ve")
                    P.ps(te, "te", gbn)
                    P.ps(te, "te", ubn)
                    for k in range(8):
                        te.matmul(gb.ap()[:], wg_bf[:, k, hc * 128:(hc + 1) * 128], xn_fm.ap()[:, k, :],
                                  start=(k == 0), stop=(k == 7))
                    for k in range(8):
                        ins = te.matmul(ub.ap()[:], wu_bf[:, k, hc * 128:(hc + 1) * 128], xn_fm.ap()[:, k, :],
                                        start=(k == 0), stop=(k == 7))
                    P.i(ins, "te")
                block.tensor(p4_teg)
                P.touch(gbn, "te"); P.touch(ubn, "te")

                def p4_seg(se, hc=hc, gb=gb):
                    P.w(se, "te")
                    P.i(se.activation(out=g_sb[:, (hc % 2) * 512:(hc % 2 + 1) * 512], in_=gb.ap()[:], func=AF.Silu), "se")
                block.scalar(p4_seg)
                P.touch(gbn, "se")

                def p4_veh(ve, hc=hc, ub=ub):
                    P.w(ve, "se")
                    P.i(ve.scalar_tensor_tensor(h_sb[:, hc, :], ub.ap()[:], 1.0 / 127.0, g_sb[:, (hc % 2) * 512:(hc % 2 + 1) * 512],
                                                op0=ALU.mult, op1=ALU.mult), "ve")
                block.vector(p4_veh)
                P.touch(ubn, "ve")

            for tch in range(2):
                tca, tcb = tch * 2, tch * 2 + 1
                ca, cb = tile * 4 + tca, tile * 4 + tcb

                def p4_syd0(sync, tch=tch):
                    if first_down[0]:
                        P.w(sync, qg)
                        P.w(sync, "ge")
                        first_down[0] = False
                    P.w(sync, "ve")
                    P.w(sync, "te")
                    P.i(sync.dma_start(out=wdb[:, 0, :], in_=wd_bf[0:128, :]), sq, 16)
                block.sync(p4_syd0)

                for hc in range(24):
                    if hc + 1 < 24:
                        def p4_syd(sync, hc=hc):
                            P.w(sync, "te")
                            P.i(sync.dma_start(out=wdb[:, (hc + 1) % 2, :],
                                               in_=wd_bf[(hc + 1) * 128:(hc + 2) * 128, :]), sq, 16)
                        block.sync(p4_syd)

                    def p4_ted(te, hc=hc, tca=tca, tcb=tcb):
                        P.w(te, sq, 16 if hc + 1 < 24 else 0)
                        if hc == 0:
                            P.ps(te, "te", "ps2"); P.ps(te, "te", "ps3")
                            P.ps(te, "te", "ps4"); P.ps(te, "te", "ps5")
                        st_, sp_ = (hc == 0), (hc == 23)
                        te.matmul(ps2.ap()[:], h_sb[:, hc, tca * 128:(tca + 1) * 128], wdb[:, hc % 2, 0:512],
                                  start=st_, stop=sp_)
                        te.matmul(ps3.ap()[:], h_sb[:, hc, tca * 128:(tca + 1) * 128], wdb[:, hc % 2, 512:1024],
                                  start=st_, stop=sp_)
                        te.matmul(ps4.ap()[:], h_sb[:, hc, tcb * 128:(tcb + 1) * 128], wdb[:, hc % 2, 0:512],
                                  start=st_, stop=sp_)
                        ins = te.matmul(ps5.ap()[:], h_sb[:, hc, tcb * 128:(tcb + 1) * 128], wdb[:, hc % 2, 512:1024],
                                        start=st_, stop=sp_)
                        P.i(ins, "te")
                    block.tensor(p4_ted)
                P.touch("ps2", "te"); P.touch("ps3", "te")
                P.touch("ps4", "te"); P.touch("ps5", "te")

                def p4_syr2(sync, ca=ca, cb=cb, tca=tca, tcb=tcb):
                    P.w(sync, "ve")
                    P.i(sync.dma_start(out=x_tm.ap()[:, 0, :], in_=r_dram[ca * 128:(ca + 1) * 128, :]), sq, 16)
                    P.i(sync.dma_start(out=x_tm.ap()[:, 1, :], in_=r_dram[cb * 128:(cb + 1) * 128, :]), sq, 16)
                block.sync(p4_syr2)

                def p4_vout(ve):
                    P.w(ve, "te")
                    P.w(ve, sq)
                    ve.tensor_add(x_tm.ap()[:, 0, 0:512], ps2.ap()[:], x_tm.ap()[:, 0, 0:512])
                    ve.tensor_add(x_tm.ap()[:, 0, 512:1024], ps3.ap()[:], x_tm.ap()[:, 0, 512:1024])
                    ve.tensor_add(x_tm.ap()[:, 1, 0:512], ps4.ap()[:], x_tm.ap()[:, 1, 0:512])
                    P.i(ve.tensor_add(x_tm.ap()[:, 1, 512:1024], ps5.ap()[:], x_tm.ap()[:, 1, 512:1024]), "ve")
                block.vector(p4_vout)
                P.touch("ps2", "ve"); P.touch("ps3", "ve")
                P.touch("ps4", "ve"); P.touch("ps5", "ve")

                def p4_syo(sync, ca=ca, cb=cb):
                    P.w(sync, "ve")
                    P.i(sync.dma_start(out=out_hbm[ca * 128:(ca + 1) * 128, :], in_=x_tm.ap()[:, 0, :]), sq, 16)
                    P.i(sync.dma_start(out=out_hbm[cb * 128:(cb + 1) * 128, :], in_=x_tm.ap()[:, 1, :]), sq, 16)
                block.sync(p4_syo)

        def tap_sy(sync):
            P.w(sync, "ve"); P.w(sync, "se"); P.w(sync, "ge")
            if "H_loc" in tap_t:
                P.i(sync.dma_start(out=tap_t["H_loc"], in_=H.ap()[:]), sq, 16)
            if "lgsum" in tap_t:
                P.i(sync.dma_start(out=tap_t["lgsum"], in_=lgcum.ap()[:]), sq, 16)
            if "sz" in tap_t:
                P.i(sync.dma_start(out=tap_t["sz"], in_=z_dram.rearrange("c p d -> (c p) d")), sq, 16)
            if "y_loc" in tap_t:
                P.i(sync.dma_start(out=tap_t["y_loc"], in_=y_dram.rearrange("c p d -> (c p) d")), sq, 16)
            if "resid" in tap_t:
                P.i(sync.dma_start(out=tap_t["resid"], in_=r_dram), sq, 16)
            if "negA" in tap_t:
                P.i(sync.dma_start(out=tap_t["negA"], in_=c_negA.ap()[:]), sq, 16)
            if "lam" in tap_t:
                P.i(sync.dma_start(out=tap_t["lam"], in_=lam_all.ap().rearrange("p c h -> p (c h)")), sq, 16)
        block.sync(tap_sy)

        def fin(sync):
            P.w(sync, sq)
            P.w(sync, qg)
        block.sync(fin)

    return nc


def _host_inputs(inputs):
    x = np.ascontiguousarray(inputs["L0_attn_x"])
    W_in = np.ascontiguousarray(inputs["L0_attn_W_in"])
    W_out = np.ascontiguousarray(inputs["L0_attn_W_out"])
    conv_w = np.ascontiguousarray(inputs["L0_attn_conv_w"]).reshape(XBC, 4)
    conv_b = np.ascontiguousarray(inputs["L0_attn_conv_b"])
    convw_fm = np.ascontiguousarray(conv_w.reshape(NOC, 128, 4).transpose(1, 0, 2))
    convb_fm = np.ascontiguousarray(conv_b.reshape(NOC, 128).T)
    wg = np.ascontiguousarray(inputs["L0_ffn_w_gate"].T)
    wu = np.ascontiguousarray(inputs["L0_ffn_w_up"].T)
    wd = np.ascontiguousarray(inputs["L0_ffn_w_down"].T)
    as16 = lambda a: np.ascontiguousarray(a.view(np.int16)[:, 0::2])

    ident = np.eye(128, dtype=np.float32).astype(ml_dtypes.bfloat16)
    tri = np.ascontiguousarray(np.tri(128, dtype=np.float32).T).astype(ml_dtypes.bfloat16)
    us = np.tri(128, k=-1, dtype=np.float32).astype(ml_dtypes.bfloat16)
    row = lambda a: np.ascontiguousarray(np.broadcast_to(
        np.asarray(a, dtype=np.float32).reshape(1, -1), (128, np.asarray(a).size)))

    maps = []
    for j in range(8):
        b, s = j // 4, j % 4
        seg = x[b, s * T_CORE:(s + 1) * T_CORE]
        halo = x[b, s * T_CORE - CH:s * T_CORE] if s > 0 else np.zeros((CH, D_MODEL), np.float32)
        xe = np.ascontiguousarray(np.concatenate([halo, seg], axis=0))
        cm = np.zeros((4, 4), np.float32)
        cb = np.full((4,), NEG, np.float32)
        for i in range(4):
            if i < s:
                cb[i] = 0.0
                for k in range(i + 1, s):
                    cm[i, k] = 1.0
        maps.append({
            "x_ext": xe, "w_in": W_in, "w_out": W_out,
            "convw": convw_fm, "convb": convb_fm,
            "alog": row(inputs["L0_attn_A_log"]), "dtb": row(inputs["L0_attn_dt_bias"]),
            "dpar": row(inputs["L0_attn_D_param"]), "nw": row(inputs["L0_attn_norm_w"]),
            "cm": row(cm), "cb": row(cb),
            "ident": ident, "tri": tri, "us": us,
            "wg_t": as16(wg), "wu_t": as16(wu), "wd_t": as16(wd),
        })
    return maps


def kernel(**inputs):
    maps = _host_inputs(inputs)
    nc = build_program()
    res = run_bass_kernel_spmd(nc, maps, core_ids=list(range(8)))
    out = np.empty((2, 8192, D_MODEL), dtype=np.float32)
    for j in range(8):
        b, s = j // 4, j % 4
        out[b, s * T_CORE:(s + 1) * T_CORE] = res.results[j]["out"]
    return out
